# version 20 (final)
# baseline (speedup 1.0000x reference)
import math
import numpy as np
import concourse.bass as bass
import concourse.mybir as mybir
from concourse.bass_utils import run_bass_kernel_spmd
from contextlib import ExitStack

F32 = mybir.dt.float32
BF16 = mybir.dt.bfloat16
I32 = mybir.dt.int32
Alu = mybir.AluOpType
Act = mybir.ActivationFunctionType

D = 1024
F = 4096
T = 2048
DK = D // 128
FK = F // 128
TT = 512
NTT = T // TT
FC = 256
NFC = F // FC
EPS = 1e-5
TWO_PI = 2.0 * math.pi
MAGIC = 12582912.0

ENGINES = ("tensor", "vector", "scalar", "gpsimd", "sync")


class Ch:
    def __init__(self, nc, es, name, step):
        self.h = es.enter_context(nc.semaphore(name))
        self.step = step
        self.count = 0
        self.name = name
        self.max_waited = 0


class Prog:
    def __init__(self):
        self.prog = {e: [] for e in ENGINES}
        self.chans = []

    def wait(self, eng, ch, cnt=None):
        v = ch.count if cnt is None else cnt
        if v <= 0:
            return
        ch.max_waited = max(ch.max_waited, v)
        self.prog[eng].append(lambda e, s=ch.h, vv=v: e.wait_ge(s, vv))

    def do(self, eng, fn, inc=None):
        if inc is None:
            self.prog[eng].append(fn)
            return None
        inc.count += inc.step

        def wrapped(e, fn=fn, s=inc.h, st=inc.step):
            fn(e).then_inc(s, st)
        self.prog[eng].append(wrapped)
        return inc.count

    def check(self):
        for c in self.chans:
            assert c.max_waited <= c.count, (
                f"channel {c.name}: waited {c.max_waited} > produced {c.count}")


def build_nc(stages=("ffn", "ssm", "ffn")):
    nc = bass.Bass("TRN2", target_bir_lowering=False)
    es = ExitStack()

    din = lambda n, s: nc.dram_tensor(n, s, F32, kind="ExternalInput").ap()
    xT = din("xT", [D, T])
    W1 = din("W1", [D, F])
    W2 = din("W2", [F, D])
    BwT = din("BwT", [D, 128])
    CwT = din("CwT", [128, D])
    DwT = din("DwT", [D, D])
    b1t_in = din("b1t", [128, 32])
    b2t_in = din("b2t", [128, 8])
    tht_in = din("tht", [128, 32])
    wrt_in = din("wrt", [128, 32])
    wit_in = din("wit", [128, 32])
    laR_in = din("laR", [128, 1])
    laI_in = din("laI", [128, 1])
    OUT = nc.dram_tensor("out", [D, T], F32, kind="ExternalOutput").ap()
    dscr = lambda n, s: nc.dram_tensor(n, s, BF16, kind="Internal").ap()
    W1S = [dscr("W1S_hi", [NFC, 128, DK * FC]), dscr("W1S_lo", [NFC, 128, DK * FC])]
    W2S = [dscr("W2S_hi", [DK, 128, F]), dscr("W2S_lo", [DK, 128, F])]
    DS = [dscr("DS_hi", [DK, 128, D]), dscr("DS_lo", [DK, 128, D])]

    sb = lambda n, s, d=F32: es.enter_context(nc.sbuf_tensor(n, s, d)).ap()
    x = sb("x", [128, DK * T])
    ws = sb("ws", [128, 24576])
    xnhi = sb("xnhi", [128, DK * TT], BF16)
    xnlo = sb("xnlo", [128, DK * TT], BF16)
    bwt_h = sb("bwt_h", [128, DK * 128], BF16)
    bwt_l = sb("bwt_l", [128, DK * 128], BF16)
    cwt_h = sb("cwt_h", [128, D], BF16)
    cwt_l = sb("cwt_l", [128, D], BF16)
    stat = sb("stat", [128, 1024])
    x2t = sb("x2t", [128, 1024])
    csl = sb("csl", [128, 1024])
    ssl = sb("ssl", [128, 1024])
    tqa = sb("tqa", [128, 512])
    tqb = sb("tqb", [128, 512])
    tbl = sb("tbl", [128, 256])
    one128 = sb("one128", [128, 1])
    one1 = sb("one1", [1, 128])

    b1t = tbl[:, 0:32]
    K1t = tbl[:, 32:64]
    K2t = tbl[:, 64:96]
    biasC = tbl[:, 96:128]
    biasS = tbl[:, 128:160]
    b2h = tbl[:, 160:168]
    aR = tbl[:, 168:169]; aI = tbl[:, 169:170]; absA = tbl[:, 170:171]
    phi2p = tbl[:, 171:172]
    t0_ = tbl[:, 172:173]; t1_ = tbl[:, 173:174]; t2_ = tbl[:, 174:175]
    t3_ = tbl[:, 175:176]; t4_ = tbl[:, 176:177]
    tht = tbl[:, 177:209]
    wrt = tbl[:, 209:241]
    wit = csl[:, 0:32]
    kqt = csl[:, 32:64]
    sth = csl[:, 64:96]
    cth = csl[:, 96:128]
    tg1 = ssl[:, 0:32]
    tg2 = ssl[:, 32:64]
    tg3 = ssl[:, 64:96]

    wsf = lambda a, b: ws[:, a:b]
    wsb = lambda a, b: ws[:, a:b].bitcast(BF16)
    qhi = wsb(0, 8192)
    qlo = wsb(8192, 16384)
    w1b = [wsb(16384 + 1024 * i, 17408 + 1024 * i) for i in range(4)]
    w2b = [wsb(20480 + 1024 * i, 21504 + 1024 * i) for i in range(4)]
    xn2hi, xn2lo = qhi, qlo
    Bu = wsf(16384, 18432)
    u_ = wsf(18432, 20480)
    g_ = wsf(20480, 22528)
    gX = wsf(22528, 24576)
    BuX = g_
    rotmp = u_
    h_ = Bu
    hhi = wsb(18432, 19456)
    hlo = wsb(19456, 20480)
    dd = [wsb(20480, 21504), wsb(21504, 22528)]
    cs_c = xnhi.bitcast(F32)
    cs_s = xnlo.bitcast(F32)
    stg = [wsf(0, 4096), wsf(4096, 8192)]
    ohi = [wsb(16384, 18432), wsb(20480, 22528)]
    olo = [wsb(18432, 20480), wsb(22528, 24576)]

    psv = [es.enter_context(nc.psum_tensor(f"p{i}", [128, 512], F32)).ap() for i in range(8)]

    P = Prog()
    ch = lambda n, s: P.chans.append(Ch(nc, es, n, s)) or P.chans[-1]
    c_sd = ch("c_sd", 16)
    c_gd = ch("c_gd", 16)
    c_tn = ch("c_tn", 1)
    c_vc = ch("c_vc", 1)
    c_ac = ch("c_ac", 1)
    c_xd = ch("c_xd", 16)
    W, E = P.wait, P.do

    bank_free = {i: 0 for i in range(8)}

    x_loaded_t = []
    for t in range(NTT):
        for dk in range(DK):
            E("scalar", lambda e, dk=dk, t=t: e.dma_start(
                out=x[:, dk * T + t * TT:dk * T + (t + 1) * TT],
                in_=xT[128 * dk:128 * (dk + 1), t * TT:(t + 1) * TT]), inc=c_xd)
        x_loaded_t.append(c_xd.count)
    x_loaded = c_xd.count
    for (dst, src) in ((b1t, b1t_in), (b2h, b2t_in), (tht, tht_in), (wrt, wrt_in),
                       (wit, wit_in), (aR, laR_in), (aI, laI_in)):
        E("sync", lambda e, d=dst, s=src: e.dma_start(out=d, in_=s[:, :]), inc=c_sd)
    small_loaded = c_sd.count

    E("vector", lambda e: e.memset(one128[:, :], 1.0))
    E("vector", lambda e: e.memset(one1[:, :], 1.0), inc=c_vc)
    ones_ready = c_vc.count

    W("scalar", c_sd, small_loaded)
    E("scalar", lambda e: e.activation(t0_, aR, Act.Exp), inc=c_ac)
    W("vector", c_ac)
    E("vector", lambda e: e.tensor_scalar(t0_, t0_, -0.5, None, Alu.mult))
    E("vector", lambda e: e.drain())
    E("vector", lambda e: e.tensor_scalar(t1_, aI, 0.5, None, Alu.mult))
    E("vector", lambda e: e.drain())
    E("vector", lambda e: e.tensor_scalar(t2_, t0_, -1.0, 1.0, Alu.mult, Alu.add))
    E("vector", lambda e: e.drain())
    E("vector", lambda e: e.tensor_tensor(t2_, t2_, t2_, Alu.mult))
    E("vector", lambda e: e.drain())
    E("vector", lambda e: e.tensor_tensor(t3_, t1_, t1_, Alu.mult))
    E("vector", lambda e: e.drain())
    E("vector", lambda e: e.tensor_tensor(t2_, t2_, t3_, Alu.add))
    E("vector", lambda e: e.drain())
    E("vector", lambda e: e.reciprocal(t2_, t2_))
    E("vector", lambda e: e.drain())
    E("vector", lambda e: e.tensor_tensor(t4_, t0_, t0_, Alu.mult))
    E("vector", lambda e: e.drain())
    E("vector", lambda e: e.tensor_tensor(t4_, t4_, t3_, Alu.add))
    E("vector", lambda e: e.drain())
    E("vector", lambda e: e.tensor_scalar(t4_, t4_, -1.0, 1.0, Alu.mult, Alu.add))
    E("vector", lambda e: e.drain())
    E("vector", lambda e: e.tensor_tensor(aR, t4_, t2_, Alu.mult))
    E("vector", lambda e: e.drain())
    E("vector", lambda e: e.tensor_tensor(t1_, t1_, t2_, Alu.mult))
    E("vector", lambda e: e.drain())
    E("vector", lambda e: e.tensor_scalar(aI, t1_, 2.0, None, Alu.mult))
    E("vector", lambda e: e.drain())
    E("vector", lambda e: e.tensor_tensor(t0_, aR, aR, Alu.mult))
    E("vector", lambda e: e.drain())
    E("vector", lambda e: e.tensor_tensor(t1_, aI, aI, Alu.mult))
    E("vector", lambda e: e.drain())
    E("vector", lambda e: e.tensor_tensor(t0_, t0_, t1_, Alu.add), inc=c_vc)
    E("vector", lambda e: e.drain())
    W("scalar", c_vc)
    E("scalar", lambda e: e.activation(absA, t0_, Act.Sqrt), inc=c_ac)
    W("vector", c_ac)
    E("vector", lambda e: e.reciprocal(t0_, aR))
    E("vector", lambda e: e.drain())
    E("vector", lambda e: e.tensor_tensor(t1_, aI, t0_, Alu.mult), inc=c_vc)
    E("vector", lambda e: e.drain())
    W("scalar", c_vc)
    E("scalar", lambda e: e.activation(t2_, t1_, Act.Arctan), inc=c_ac)
    E("scalar", lambda e: e.activation(t3_, aI, Act.Sign), inc=c_ac)
    W("vector", c_ac)
    E("vector", lambda e: e.tensor_scalar(t4_, aR, 0.0, None, Alu.is_lt))
    E("vector", lambda e: e.drain())
    E("vector", lambda e: e.tensor_tensor(t4_, t4_, t3_, Alu.mult))
    E("vector", lambda e: e.drain())
    E("vector", lambda e: e.scalar_tensor_tensor(t2_, t4_, math.pi, t2_, Alu.mult, Alu.add))
    E("vector", lambda e: e.drain())
    E("vector", lambda e: e.tensor_scalar(phi2p, t2_, 1.0 / TWO_PI, None, Alu.mult), inc=c_vc)
    E("vector", lambda e: e.drain())

    def round_reduce(dst, srcf):
        tgi = tg1.bitcast(I32)
        E("vector", lambda e: e.tensor_scalar(tg2, srcf, 1.0, 16384.0, Alu.mult, Alu.add))
        E("vector", lambda e: e.drain())
        E("vector", lambda e: e.tensor_copy(tgi, tg2))
        E("vector", lambda e: e.drain())
        E("vector", lambda e: e.tensor_copy(tg2, tgi))
        E("vector", lambda e: e.drain())
        E("vector", lambda e: e.tensor_scalar(tg2, tg2, 1.0, -16384.0, Alu.mult, Alu.add))
        E("vector", lambda e: e.drain())
        E("vector", lambda e: e.scalar_tensor_tensor(tg2, tg2, -1.0, srcf,
                                                     Alu.mult, Alu.add))
        E("vector", lambda e: e.drain())
        E("vector", lambda e: e.tensor_scalar(dst, tg2, TWO_PI, None, Alu.mult))
        E("vector", lambda e: e.drain())

    W("vector", c_sd, small_loaded)
    E("vector", lambda e: e.tensor_scalar(tg3, tht, 1.0 / TWO_PI, None, Alu.mult))
    E("vector", lambda e: e.drain())
    round_reduce(sth, tg3)
    E("vector", lambda e: e.tensor_scalar(tg3, tg3, 1.0, 0.25, Alu.mult, Alu.add))
    E("vector", lambda e: e.drain())
    round_reduce(cth, tg3)
    E("vector", lambda e: e.nop(), inc=c_vc)
    E("vector", lambda e: e.drain())
    W("scalar", c_vc)
    E("scalar", lambda e: e.activation(sth, sth, Act.Sin))
    E("scalar", lambda e: e.activation(cth, cth, Act.Sin), inc=c_ac)
    W("vector", c_ac)
    E("vector", lambda e: e.tensor_tensor(tg1, cth, wrt, Alu.mult))
    E("vector", lambda e: e.drain())
    E("vector", lambda e: e.tensor_tensor(tg2, sth, wit, Alu.mult))
    E("vector", lambda e: e.drain())
    E("vector", lambda e: e.tensor_tensor(tg1, tg1, tg2, Alu.add))
    E("vector", lambda e: e.drain())
    E("vector", lambda e: e.tensor_tensor(tg1, tg1, wit, Alu.mult))
    E("vector", lambda e: e.drain())
    E("vector", lambda e: e.tensor_scalar(kqt, tg1, 0.1, None, Alu.mult))
    E("vector", lambda e: e.drain())
    E("vector", lambda e: e.tensor_tensor(K1t, kqt, sth, Alu.mult))
    E("vector", lambda e: e.drain())
    E("vector", lambda e: e.tensor_tensor(K2t, kqt, cth, Alu.mult))
    E("vector", lambda e: e.drain())
    E("vector", lambda e: e.tensor_scalar(biasC, b1t, 0.1, math.pi / 2, Alu.mult, Alu.add))
    E("vector", lambda e: e.drain())
    E("vector", lambda e: e.tensor_scalar(biasS, b1t, 0.1, None, Alu.mult))
    E("vector", lambda e: e.drain())
    E("vector", lambda e: e.tensor_scalar(b2h, b2h, 0.5, None, Alu.mult), inc=c_vc)
    E("vector", lambda e: e.drain())
    qtables_done = c_vc.count

    units = []
    for fc in range(NFC):
        src = W1[:, fc * FC:(fc + 1) * FC].rearrange("(dk p) f -> p dk f", p=128)
        units.append(("w1", fc, src, DK * FC, DK))
    for dk in range(DK):
        src = W2[:, dk * 128:(dk + 1) * 128].rearrange("(fs p) d -> p fs d", p=128)
        units.append(("w2", dk, src, F, FK))
    units.append(("bwt", 0, BwT[:, :].rearrange("(dk p) n -> p dk n", p=128), DK * 128, DK))
    units.append(("cwt", 0, CwT[:, :].rearrange("n (a d2) -> n a d2", a=1), D, 1))
    for ek in range(DK):
        src = DwT[:, ek * 128:(ek + 1) * 128].rearrange("(dk p) e2 -> p dk e2", p=128)
        units.append(("dwt", ek, src, D, DK))

    vec_split = {}
    wb_done = {}
    for i, (kind, idx, src, n, mid) in enumerate(units):
        s = i % 2
        if i >= 2:
            W("sync", c_vc, vec_split[i - 2])
        E("sync", lambda e, src=src, s=s, n=n, mid=mid: e.dma_start(
            out=stg[s][:, 0:n].rearrange("p (a b) -> p a b", a=mid), in_=src),
          inc=c_sd)
        ld = c_sd.count
        W("vector", c_sd, ld)
        if i >= 2 and units[i - 2][0] not in ("bwt", "cwt"):
            W("vector", c_gd, wb_done[i - 2])
        if kind == "bwt":
            dsth, dstl = bwt_h[:, :], bwt_l[:, :]
        elif kind == "cwt":
            dsth, dstl = cwt_h[:, :], cwt_l[:, :]
        else:
            dsth, dstl = ohi[s][:, 0:n], olo[s][:, 0:n]
        E("vector", lambda e, d=dsth, s=s, n=n: e.tensor_copy(d, stg[s][:, 0:n]))
        E("vector", lambda e, dh=dsth, dl=dstl, s=s, n=n:
          e.scalar_tensor_tensor(dl, dh, -1.0, stg[s][:, 0:n], Alu.mult, Alu.add),
          inc=c_vc)
        vec_split[i] = c_vc.count
        if kind not in ("bwt", "cwt"):
            W("gpsimd", c_vc, vec_split[i])
            scr = {"w1": W1S, "w2": W2S, "dwt": DS}[kind]
            outs = (scr[0][idx], scr[1][idx])
            E("gpsimd", lambda e, o=outs[0], s=s, n=n: e.dma_start(out=o[:, :],
                                                                   in_=ohi[s][:, 0:n]), inc=c_gd)
            E("gpsimd", lambda e, o=outs[1], s=s, n=n: e.dma_start(out=o[:, :],
                                                                   in_=olo[s][:, 0:n]), inc=c_gd)
            wb_done[i] = c_gd.count
    w1_ready = [wb_done[i] for i, un in enumerate(units) if un[0] == "w1"]
    w2_ready = [wb_done[i] for i, un in enumerate(units) if un[0] == "w2"]
    d_ready = [wb_done[i] for i, un in enumerate(units) if un[0] == "dwt"]
    bwt_ready = [vec_split[i] for i, un in enumerate(units) if un[0] == "bwt"][0]
    cwt_ready = [vec_split[i] for i, un in enumerate(units) if un[0] == "cwt"][0]
    startup_sd_all = c_sd.count
    startup_gd_all = c_gd.count

    w1slot_free = [0, 0]
    w2slot_free = [0, 0]
    ddslot_free = [0, 0]
    sqslot_free = [0, 0]
    csslot_free = [0, 0]
    state = {"last_split_wb": c_gd.count}

    def ln_stats(t, wait_scalar_x, sq_vector=False):
        ts_ = t * TT
        mslot = stat[0:1, 0:512]
        rslot = stat[0:1, 512:1024]
        tmp1 = tqa[0:1, 0:512]
        W("tensor", c_xd, x_loaded_t[t])
        W("tensor", c_vc, max(bank_free[6], bank_free[7]))
        for dk in range(DK):
            s = dk % 2
            sqe = "vector" if sq_vector else "scalar"
            W(sqe, c_tn, sqslot_free[s])
            W(sqe, c_vc, wait_scalar_x)
            W(sqe, c_xd, x_loaded_t[t])
            if sq_vector:
                E("vector", lambda e, dk=dk, s=s: e.tensor_mul(
                    x2t[:, s * 512:(s + 1) * 512], x[:, dk * T + ts_:dk * T + ts_ + TT],
                    x[:, dk * T + ts_:dk * T + ts_ + TT]), inc=c_vc)
                sqch, sqv = c_vc, c_vc.count
            else:
                E("scalar", lambda e, dk=dk, s=s: e.activation(
                    x2t[:, s * 512:(s + 1) * 512], x[:, dk * T + ts_:dk * T + ts_ + TT],
                    Act.Square), inc=c_ac)
                sqch, sqv = c_ac, c_ac.count
            E("tensor", lambda e, dk=dk: e.matmul(
                psv[6][0:1, :], one128[:, :], x[:, dk * T + ts_:dk * T + ts_ + TT],
                start=(dk == 0), stop=(dk == DK - 1), skip_group_check=True))
            W("tensor", sqch, sqv)
            E("tensor", lambda e, s=s, dk=dk: e.matmul(
                psv[7][0:1, :], one128[:, :], x2t[:, s * 512:(s + 1) * 512],
                start=(dk == 0), stop=(dk == DK - 1), skip_group_check=True), inc=c_tn)
            sqslot_free[s] = c_tn.count
        stats_mm = c_tn.count
        W("vector", c_tn, stats_mm)
        E("vector", lambda e, m=mslot: e.tensor_scalar(m, psv[6][0:1, :], 1.0 / D, None, Alu.mult))
        E("vector", lambda e, r=rslot: e.tensor_scalar(r, psv[7][0:1, :], 1.0 / D, None, Alu.mult))
        E("vector", lambda e, m=mslot: e.tensor_tensor(tmp1, m, m, Alu.mult))
        E("vector", lambda e, r=rslot: e.tensor_tensor(r, r, tmp1, Alu.subtract))
        E("vector", lambda e, r=rslot: e.tensor_scalar(r, r, EPS, None, Alu.add), inc=c_vc)
        W("scalar", c_vc)
        E("scalar", lambda e, r=rslot: e.activation(r, r, Act.Sqrt), inc=c_ac)
        W("vector", c_ac)
        E("vector", lambda e, r=rslot: e.reciprocal(r, r), inc=c_vc)
        return c_vc.count

    def ln_apply(t, r_done, xn_hi_dk, xn_lo_dk, wait_vec_xnwrite):
        ts_ = t * TT
        mslot = stat[0:1, 0:512]
        rslot = stat[0:1, 512:1024]
        W("tensor", c_vc, r_done)
        E("tensor", lambda e, m=mslot: e.matmul(psv[6][:, :], one1[:, :], m,
                                                start=True, stop=True, skip_group_check=True))
        E("tensor", lambda e, r=rslot: e.matmul(psv[7][:, :], one1[:, :], r,
                                                start=True, stop=True, skip_group_check=True),
          inc=c_tn)
        bcast_done = c_tn.count
        W("vector", c_tn, bcast_done)
        W("vector", c_tn, wait_vec_xnwrite)
        for dk in range(DK):
            E("vector", lambda e, dk=dk: e.tensor_tensor(
                tqa[:, :], x[:, dk * T + ts_:dk * T + ts_ + TT], psv[6][:, :], Alu.subtract))
            E("vector", lambda e: e.tensor_tensor(tqa[:, :], tqa[:, :], psv[7][:, :], Alu.mult))
            E("vector", lambda e, dk=dk: e.tensor_copy(xn_hi_dk(dk), tqa[:, :]))
            E("vector", lambda e, dk=dk: e.scalar_tensor_tensor(
                xn_lo_dk(dk), xn_hi_dk(dk), -1.0, tqa[:, :], Alu.mult, Alu.add), inc=c_vc)
        bank_free[6] = c_vc.count
        bank_free[7] = c_vc.count
        return c_vc.count

    def emit_ln(t, xn_hi_dk, xn_lo_dk, wait_vec_xnwrite, wait_scalar_x):
        r_done = ln_stats(t, wait_scalar_x)
        return ln_apply(t, r_done, xn_hi_dk, xn_lo_dk, wait_vec_xnwrite)

    def emit_ffn(tables_hook=None):
        first_call = state["first_ffn"]
        state["first_ffn"] = False
        xn_hi_dk = lambda dk: xnhi[:, dk * TT:(dk + 1) * TT]
        xn_lo_dk = lambda dk: xnlo[:, dk * TT:(dk + 1) * TT]
        if state.get("pre_xn0") is not None:
            xn_ready = state.pop("pre_xn0")
        else:
            xn_ready = emit_ln(0, xn_hi_dk, xn_lo_dk,
                               state["xn_reader"], state["x_done"])
        if first_call:
            W("sync", c_gd, startup_gd_all)
            W("vector", c_sd, startup_sd_all)
            W("vector", c_gd, startup_gd_all)
        W("sync", c_tn, state["xn_reader"])
        for t in range(NTT):
            ts_ = t * TT
            a_first_group = None
            for fc in range(NFC):
                s = fc % 2
                W("sync", c_tn, w1slot_free[s])
                W("sync", c_gd, w1_ready[fc])
                W("sync", c_vc, xn_ready)
                E("sync", lambda e, fc=fc, s=s: e.dma_start(out=w1b[2 * s][:, :],
                                                            in_=W1S[0][fc][:, :]), inc=c_sd)
                E("sync", lambda e, fc=fc, s=s: e.dma_start(out=w1b[2 * s + 1][:, :],
                                                            in_=W1S[1][fc][:, :]), inc=c_sd)
                w1_ld = c_sd.count
                for fs in range(2):
                    sidx = fc * 2 + fs
                    bank = 2 + (sidx % 4)
                    cslot = sidx % 2
                    W("tensor", c_sd, w1_ld)
                    W("tensor", c_vc, xn_ready)
                    W("tensor", c_vc, bank_free[bank])
                    for p in range(3):
                        lh = w1b[2 * s] if p < 2 else w1b[2 * s + 1]
                        rh = xn_hi_dk if p != 1 else xn_lo_dk
                        for dk in range(DK):
                            lhs = lh[:, dk * FC + fs * 128:dk * FC + fs * 128 + 128]
                            first = (p == 0 and dk == 0)
                            last = (p == 2 and dk == DK - 1)
                            E("tensor", lambda e, lhs=lhs, r=rh(dk), b=bank, f=first, l=last:
                              e.matmul(psv[b][:, :], lhs, r, start=f, stop=l,
                                       skip_group_check=True),
                              inc=(c_tn if last else None))
                    mm_done = c_tn.count
                    if a_first_group is None:
                        a_first_group = mm_done
                    w1slot_free[s] = mm_done
                    W("scalar", c_tn, mm_done)
                    W("scalar", c_vc, csslot_free[cslot])
                    E("scalar", lambda e, b=bank, c2=cslot, i=sidx: e.activation(
                        csl[:, c2 * 512:(c2 + 1) * 512], psv[b][:, :], Act.Sin,
                        bias=biasC[:, i:i + 1], scale=0.1))
                    E("scalar", lambda e, b=bank, c2=cslot, i=sidx: e.activation(
                        ssl[:, c2 * 512:(c2 + 1) * 512], psv[b][:, :], Act.Sin,
                        bias=biasS[:, i:i + 1], scale=0.1), inc=c_ac)
                    sins_done = c_ac.count
                    W("vector", c_ac, sins_done)
                    cap = csl[:, cslot * 512:(cslot + 1) * 512]
                    sap = ssl[:, cslot * 512:(cslot + 1) * 512]
                    E("vector", lambda e, cap=cap, i=sidx: e.tensor_scalar(
                        tqa[:, :], cap, K1t[:, i:i + 1], b1t[:, i:i + 1], Alu.mult, Alu.add))
                    E("vector", lambda e, sap=sap, i=sidx: e.scalar_tensor_tensor(
                        tqa[:, :], sap, K2t[:, i:i + 1], tqa[:, :], Alu.mult, Alu.add),
                      inc=c_vc)
                    csslot_free[cslot] = c_vc.count
                    E("vector", lambda e, b=bank: e.tensor_tensor(
                        tqb[:, :], tqa[:, :], psv[b][:, :], Alu.add), inc=c_vc)
                    bank_free[bank] = c_vc.count
                    E("vector", lambda e: e.tensor_scalar(tqb[:, :], tqb[:, :], 0.0, None,
                                                          Alu.max))
                    qh = qhi[:, sidx * TT:(sidx + 1) * TT]
                    ql = qlo[:, sidx * TT:(sidx + 1) * TT]
                    E("vector", lambda e, qh=qh: e.tensor_copy(qh, tqb[:, :]))
                    E("vector", lambda e, qh=qh, ql=ql: e.scalar_tensor_tensor(
                        ql, qh, -1.0, tqb[:, :], Alu.mult, Alu.add), inc=c_vc)
            q_ready = c_vc.count
            a_last_mm = c_tn.count
            if t < NTT - 1:
                r_done_next = ln_stats(t + 1, state["x_done"])
            for dk in range(DK):
                bank = dk % 6
                if t < NTT - 1 and dk == 1:
                    xn_ready = ln_apply(t + 1, r_done_next, xn_hi_dk, xn_lo_dk, a_last_mm)
                if t == NTT - 1 and dk == 3 and tables_hook is not None:
                    tables_hook(a_last_mm)
                W("tensor", c_vc, bank_free[bank])
                for h in range(2):
                    un = dk * 2 + h
                    bp = un % 2
                    W("sync", c_tn, w2slot_free[bp])
                    W("sync", c_gd, w2_ready[dk])
                    W("sync", c_vc, state["w2region_guard"])
                    E("sync", lambda e, dk=dk, h=h, bp=bp: e.dma_start(
                        out=w2b[2 * bp][:, :], in_=W2S[0][dk][:, h * 2048:(h + 1) * 2048]),
                      inc=c_sd)
                    E("sync", lambda e, dk=dk, h=h, bp=bp: e.dma_start(
                        out=w2b[2 * bp + 1][:, :], in_=W2S[1][dk][:, h * 2048:(h + 1) * 2048]),
                      inc=c_sd)
                    w2_ld = c_sd.count
                    W("tensor", c_sd, w2_ld)
                    W("tensor", c_vc, q_ready)
                    for fs in range(h * 16, h * 16 + 16):
                        o = (fs - h * 16) * 128
                        for p in range(3):
                            lhs = (w2b[2 * bp] if p < 2 else w2b[2 * bp + 1])[:, o:o + 128]
                            rh = (qhi if p != 1 else qlo)[:, fs * TT:(fs + 1) * TT]
                            first = (h == 0 and fs == 0 and p == 0)
                            stop = (h == 1 and fs == h * 16 + 15 and p == 2)
                            unit_end = (fs == h * 16 + 15 and p == 2)
                            E("tensor", lambda e, lhs=lhs, rh=rh, b=bank, f=first, l=stop:
                              e.matmul(psv[b][:, :], lhs, rh, start=f, stop=l,
                                       skip_group_check=True),
                              inc=(c_tn if unit_end else None))
                    w2slot_free[bp] = c_tn.count
                b_done = c_tn.count
                W("vector", c_tn, b_done)
                E("vector", lambda e, b=bank, dk=dk: e.tensor_scalar(
                    tqb[:, :], psv[b][:, :], 0.5, b2h[:, dk:dk + 1], Alu.mult, Alu.add))
                E("vector", lambda e, dk=dk, ts_=ts_: e.tensor_tensor(
                    x[:, dk * T + ts_:dk * T + ts_ + TT],
                    x[:, dk * T + ts_:dk * T + ts_ + TT], tqb[:, :], Alu.add), inc=c_vc)
                bank_free[bank] = c_vc.count
        state["x_done"] = c_vc.count
        state["xn_reader"] = c_tn.count
        state["w2region_guard"] = 0

    def emit_scan_tables(wait_tn):
        tmp1 = wsf(16384, 18432)
        tmp2 = wsf(18432, 20480)
        W("gpsimd", c_tn, wait_tn)
        E("gpsimd", lambda e: e.iota(tmp1[:, :], [[1, T]], base=0, channel_multiplier=0,
                                     allow_small_or_imprecise_dtypes=True), inc=c_gd)
        iota_done = c_gd.count
        W("vector", c_gd, iota_done)
        W("vector", c_tn, wait_tn)
        csi = xnhi.bitcast(I32)
        E("vector", lambda e: e.tensor_scalar(tmp2[:, :], tmp1[:, :], phi2p[:, 0:1], None,
                                              Alu.mult))
        E("vector", lambda e: e.tensor_scalar(tmp1[:, :], tmp2[:, :], 1.0, 16384.0,
                                              Alu.mult, Alu.add))
        E("vector", lambda e: e.tensor_copy(csi[:, :], tmp1[:, :]))
        E("vector", lambda e: e.tensor_copy(tmp1[:, :], csi[:, :]))
        E("vector", lambda e: e.tensor_scalar(tmp1[:, :], tmp1[:, :], 1.0, -16384.0,
                                              Alu.mult, Alu.add))
        E("vector", lambda e: e.scalar_tensor_tensor(tmp1[:, :], tmp1[:, :], -1.0,
                                                     tmp2[:, :], Alu.mult, Alu.add))
        E("vector", lambda e: e.tensor_scalar(cs_s[:, :], tmp1[:, :], TWO_PI, None,
                                              Alu.mult))
        E("vector", lambda e: e.tensor_scalar(tmp2[:, :], tmp2[:, :], 1.0, 0.25,
                                              Alu.mult, Alu.add))
        E("vector", lambda e: e.tensor_scalar(tmp1[:, :], tmp2[:, :], 1.0, 16384.0,
                                              Alu.mult, Alu.add))
        E("vector", lambda e: e.tensor_copy(csi[:, :], tmp1[:, :]))
        E("vector", lambda e: e.tensor_copy(tmp1[:, :], csi[:, :]))
        E("vector", lambda e: e.tensor_scalar(tmp1[:, :], tmp1[:, :], 1.0, -16384.0,
                                              Alu.mult, Alu.add))
        E("vector", lambda e: e.scalar_tensor_tensor(tmp1[:, :], tmp1[:, :], -1.0,
                                                     tmp2[:, :], Alu.mult, Alu.add))
        E("vector", lambda e: e.tensor_scalar(cs_c[:, :], tmp1[:, :], TWO_PI, None,
                                              Alu.mult),
          inc=c_vc)
        args_ready = c_vc.count
        W("scalar", c_vc, args_ready)
        E("scalar", lambda e: e.activation(cs_s[:, :], cs_s[:, :], Act.Sin))
        E("scalar", lambda e: e.activation(cs_c[:, :], cs_c[:, :], Act.Sin), inc=c_ac)
        state["cs_done"] = c_ac.count
        state["tables_emitted"] = True

    def emit_ssm(dump=False):
        ssm_start_vc = c_vc.count
        xn2_hi_dk = lambda dk, ts_: xn2hi[:, dk * T + ts_:dk * T + ts_ + TT]
        xn2_lo_dk = lambda dk, ts_: xn2lo[:, dk * T + ts_:dk * T + ts_ + TT]

        if not state.get("tables_emitted"):
            emit_scan_tables(state["xn_reader"])
        cs_done = state["cs_done"]

        def bproj(t, xn2_rdy):
            ts_ = t * TT
            bank = t % 2
            W("tensor", c_vc, bank_free[bank])
            W("tensor", c_vc, bwt_ready)
            W("tensor", c_vc, xn2_rdy)
            for p in range(3):
                lh = bwt_h if p < 2 else bwt_l
                rh = xn2_hi_dk if p != 1 else xn2_lo_dk
                for dk in range(DK):
                    first = (p == 0 and dk == 0)
                    last = (p == 2 and dk == DK - 1)
                    E("tensor", lambda e, l2=lh[:, dk * 128:(dk + 1) * 128],
                      r=rh(dk, ts_), b=bank, f=first, l=last:
                      e.matmul(psv[b][:, :], l2, r, start=f, stop=l, skip_group_check=True),
                      inc=(c_tn if last else None))
            bu_mm = c_tn.count
            W("vector", c_tn, bu_mm)
            E("vector", lambda e, b=bank, ts_=ts_: e.tensor_copy(
                Bu[:, ts_:ts_ + TT], psv[b][:, :]), inc=c_vc)
            bank_free[bank] = c_vc.count

        hi0 = lambda t: (lambda dk, ts_=t * TT: xn2_hi_dk(dk, ts_))
        lo0 = lambda t: (lambda dk, ts_=t * TT: xn2_lo_dk(dk, ts_))
        xn_rdy = {}
        r_d = ln_stats(0, state["x_done"], sq_vector=True)
        xn_rdy[0] = ln_apply(0, r_d, hi0(0), lo0(0), state["xn_reader"])
        r_next = ln_stats(1, state["x_done"], sq_vector=True)
        for t in range(NTT):
            bproj(t, xn_rdy[t])
            if t < NTT - 1:
                xn_rdy[t + 1] = ln_apply(t + 1, r_next, hi0(t + 1), lo0(t + 1),
                                         state["xn_reader"])
                if t < NTT - 2:
                    r_next = ln_stats(t + 2, state["x_done"], sq_vector=True)
        bu_all = c_vc.count

        if dump:
            E("vector", lambda e: e.tensor_copy(x[:, 2048:4096], Bu[:, :]), inc=c_vc)
        W("gpsimd", c_vc, bu_all)
        E("gpsimd", lambda e: e.dma_start(out=BuX[0:64, :], in_=Bu[64:128, :]), inc=c_gd)
        E("gpsimd", lambda e: e.dma_start(out=BuX[64:128, :], in_=Bu[0:64, :]), inc=c_gd)
        swap1 = c_gd.count
        W("vector", c_gd, swap1)
        W("vector", c_ac, cs_done)
        E("vector", lambda e: e.tensor_tensor(u_[:, :], Bu[:, :], cs_c[:, :], Alu.mult))
        E("vector", lambda e: e.tensor_tensor(gX[:, :], BuX[:, :], cs_s[:, :], Alu.mult))
        E("vector", lambda e: e.tensor_tensor(u_[0:64, :], u_[0:64, :], gX[0:64, :], Alu.add))
        E("vector", lambda e: e.tensor_tensor(u_[64:128, :], u_[64:128, :], gX[64:128, :],
                                              Alu.subtract))
        E("vector", lambda e: e.tensor_tensor_scan(
            g_[:, :], absA[:, 0:1].broadcast_to((128, T)), u_[:, :], 0.0,
            Alu.mult, Alu.add), inc=c_vc)
        scan_done = c_vc.count
        W("gpsimd", c_vc, scan_done)
        E("gpsimd", lambda e: e.dma_start(out=gX[0:64, :], in_=g_[64:128, :]), inc=c_gd)
        E("gpsimd", lambda e: e.dma_start(out=gX[64:128, :], in_=g_[0:64, :]), inc=c_gd)
        swap2 = c_gd.count
        W("vector", c_gd, swap2)
        E("vector", lambda e: e.tensor_tensor(rotmp[:, :], gX[:, :], cs_s[:, :], Alu.mult))
        E("vector", lambda e: e.tensor_tensor(h_[:, :], g_[:, :], cs_c[:, :], Alu.mult))
        E("vector", lambda e: e.tensor_tensor(h_[0:64, :], h_[0:64, :], rotmp[0:64, :],
                                              Alu.subtract))
        E("vector", lambda e: e.tensor_tensor(h_[64:128, :], h_[64:128, :], rotmp[64:128, :],
                                              Alu.add), inc=c_vc)
        rot_out_done = c_vc.count
        if dump:
            E("vector", lambda e: e.tensor_copy(x[:, 0:2048], h_[:, :]))
            E("vector", lambda e: e.tensor_copy(x[:, 4096:6144], cs_c[:, :]))
            E("vector", lambda e: e.tensor_copy(x[:, 6144:8192], cs_s[:, :]), inc=c_vc)
            state["x_done"] = c_vc.count
            state["xn_reader"] = c_tn.count
            state["w2region_guard"] = c_vc.count
            return
        E("vector", lambda e: e.tensor_copy(hhi[:, :], h_[:, :]))
        E("vector", lambda e: e.scalar_tensor_tensor(hlo[:, :], hhi[:, :], -1.0, h_[:, :],
                                                     Alu.mult, Alu.add), inc=c_vc)
        h_split = c_vc.count

        dd2 = [csl[:, :].bitcast(BF16), ssl[:, :].bitcast(BF16)]
        pairs = [(t, ek) for t in range(NTT) for ek in range(DK)]

        def dpart(i):
            t, ek = pairs[i]
            ts_ = t * TT
            sl = i % 2
            bank = i % 6
            W("sync", c_tn, ddslot_free[sl])
            W("sync", c_vc, ssm_start_vc)
            W("sync", c_gd, d_ready[ek])
            E("sync", lambda e, ek=ek, sl=sl: e.dma_start(
                out=dd2[sl][:, 0:1024], in_=DS[0][ek][:, :]), inc=c_sd)
            E("sync", lambda e, ek=ek, sl=sl: e.dma_start(
                out=dd2[sl][:, 1024:2048], in_=DS[1][ek][:, :]), inc=c_sd)
            d_ld = c_sd.count
            W("tensor", c_vc, bank_free[bank])
            W("tensor", c_sd, d_ld)
            for p in range(3):
                off = 0 if p < 2 else 1024
                rh = xn2_hi_dk if p != 1 else xn2_lo_dk
                for dk in range(DK):
                    E("tensor", lambda e, l2=dd2[sl][:, off + dk * 128:off + (dk + 1) * 128],
                      r=rh(dk, ts_), b=bank, f=(p == 0 and dk == 0):
                      e.matmul(psv[b][:, :], l2, r, start=f, stop=False,
                               skip_group_check=True),
                      inc=(c_tn if (p == 2 and dk == DK - 1) else None))
            ddslot_free[sl] = c_tn.count

        def cpart(i):
            t, ek = pairs[i]
            ts_ = t * TT
            bank = i % 6
            W("tensor", c_vc, max(h_split, cwt_ready))
            for p in range(3):
                lh = cwt_h if p < 2 else cwt_l
                rh2 = hhi if p != 1 else hlo
                E("tensor", lambda e, l2=lh[:, ek * 128:(ek + 1) * 128],
                  r=rh2[:, ts_:ts_ + TT], b=bank, l=(p == 2):
                  e.matmul(psv[b][:, :], l2, r, start=False, stop=l,
                           skip_group_check=True),
                  inc=(c_tn if p == 2 else None))
            cd_mm = c_tn.count
            W("vector", c_tn, cd_mm)
            E("vector", lambda e, b=bank, ek=ek, ts_=ts_: e.tensor_tensor(
                x[:, ek * T + ts_:ek * T + ts_ + TT],
                x[:, ek * T + ts_:ek * T + ts_ + TT], psv[b][:, :], Alu.add), inc=c_vc)
            bank_free[bank] = c_vc.count

        DEPTH = 2
        for i in range(min(DEPTH, len(pairs))):
            dpart(i)
        for i in range(len(pairs)):
            cpart(i)
            if i + DEPTH < len(pairs):
                dpart(i + DEPTH)
            if i == DK - 1:
                r0f = ln_stats(0, c_vc.count)
                state["pre_xn0"] = ln_apply(
                    0, r0f, lambda dk: xnhi[:, dk * TT:(dk + 1) * TT],
                    lambda dk: xnlo[:, dk * TT:(dk + 1) * TT], 0)
        state["x_done"] = c_vc.count
        state["xn_reader"] = c_tn.count
        state["w2region_guard"] = rot_out_done
    state["x_done"] = 0
    csslot_free[0] = qtables_done
    csslot_free[1] = qtables_done
    state["xn_reader"] = 0
    state["w2region_guard"] = 0
    state["first_ffn"] = True
    state["cs_done"] = 0
    state["tables_emitted"] = False
    if stages and stages[0] == "ffn":
        _r0 = ln_stats(0, 0)
        state["pre_xn0"] = ln_apply(0, _r0, lambda dk: xnhi[:, dk * TT:(dk + 1) * TT],
                                    lambda dk: xnlo[:, dk * TT:(dk + 1) * TT], 0)
    first_stage = True
    for st_ in stages:
        if st_ == "ffn":
            hook = emit_scan_tables if (first_stage and "ssm" in stages) else None
            emit_ffn(tables_hook=hook)
            first_stage = False
        elif st_ == "ssm":
            emit_ssm()
        elif st_ == "ssmh":
            emit_ssm(dump=True)
        elif st_ == "ln":
            xn_hi_dk = lambda dk: xnhi[:, dk * TT:(dk + 1) * TT]
            xn_lo_dk = lambda dk: xnlo[:, dk * TT:(dk + 1) * TT]
            for t in range(NTT):
                ts_ = t * TT
                xr = emit_ln(t, xn_hi_dk, xn_lo_dk, state["xn_reader"], state["x_done"])
                for dk in range(DK):
                    E("vector", lambda e, dk=dk, ts_=ts_: e.tensor_tensor(
                        x[:, dk * T + ts_:dk * T + ts_ + TT], xn_hi_dk(dk), xn_lo_dk(dk),
                        Alu.add), inc=c_vc)
                state["xn_reader"] = c_tn.count
                state["x_done"] = c_vc.count
            state["x_done"] = c_vc.count
        elif st_ == "ffnq":
            xn_hi_dk = lambda dk: xnhi[:, dk * TT:(dk + 1) * TT]
            xn_lo_dk = lambda dk: xnlo[:, dk * TT:(dk + 1) * TT]
            xn_ready = emit_ln(0, xn_hi_dk, xn_lo_dk, state["xn_reader"], state["x_done"])
            W("sync", c_gd, startup_gd_all)
            W("vector", c_sd, startup_sd_all)
            W("vector", c_gd, startup_gd_all)
            for fc in range(NFC):
                s = fc % 2
                W("sync", c_tn, w1slot_free[s])
                W("sync", c_gd, w1_ready[fc])
                W("sync", c_vc, xn_ready)
                E("sync", lambda e, fc=fc, s=s: e.dma_start(out=w1b[2 * s][:, :],
                                                            in_=W1S[0][fc][:, :]), inc=c_sd)
                E("sync", lambda e, fc=fc, s=s: e.dma_start(out=w1b[2 * s + 1][:, :],
                                                            in_=W1S[1][fc][:, :]), inc=c_sd)
                w1_ld = c_sd.count
                for fs in range(2):
                    sidx = fc * 2 + fs
                    bank = 2 + (sidx % 4)
                    cslot = sidx % 2
                    W("tensor", c_sd, w1_ld)
                    W("tensor", c_vc, xn_ready)
                    W("tensor", c_vc, bank_free[bank])
                    for p in range(3):
                        lh = w1b[2 * s] if p < 2 else w1b[2 * s + 1]
                        rh = xn_hi_dk if p != 1 else xn_lo_dk
                        for dk in range(DK):
                            lhs = lh[:, dk * FC + fs * 128:dk * FC + fs * 128 + 128]
                            first = (p == 0 and dk == 0)
                            last = (p == 2 and dk == DK - 1)
                            E("tensor", lambda e, lhs=lhs, r=rh(dk), b=bank, f=first, l=last:
                              e.matmul(psv[b][:, :], lhs, r, start=f, stop=l,
                                       skip_group_check=True),
                              inc=(c_tn if last else None))
                    mm_done = c_tn.count
                    w1slot_free[s] = mm_done
                    W("scalar", c_tn, mm_done)
                    W("scalar", c_vc, csslot_free[cslot])
                    E("scalar", lambda e, b=bank, c2=cslot, i=sidx: e.activation(
                        csl[:, c2 * 512:(c2 + 1) * 512], psv[b][:, :], Act.Sin,
                        bias=biasC[:, i:i + 1], scale=0.1))
                    E("scalar", lambda e, b=bank, c2=cslot, i=sidx: e.activation(
                        ssl[:, c2 * 512:(c2 + 1) * 512], psv[b][:, :], Act.Sin,
                        bias=biasS[:, i:i + 1], scale=0.1), inc=c_ac)
                    sins_done = c_ac.count
                    W("vector", c_ac, sins_done)
                    cap = csl[:, cslot * 512:(cslot + 1) * 512]
                    sap = ssl[:, cslot * 512:(cslot + 1) * 512]
                    E("vector", lambda e, cap=cap, i=sidx: e.tensor_scalar(
                        tqa[:, :], cap, K1t[:, i:i + 1], b1t[:, i:i + 1], Alu.mult, Alu.add))
                    E("vector", lambda e, sap=sap, i=sidx: e.scalar_tensor_tensor(
                        tqa[:, :], sap, K2t[:, i:i + 1], tqa[:, :], Alu.mult, Alu.add),
                      inc=c_vc)
                    csslot_free[cslot] = c_vc.count
                    E("vector", lambda e, b=bank: e.tensor_tensor(
                        tqb[:, :], tqa[:, :], psv[b][:, :], Alu.add), inc=c_vc)
                    bank_free[bank] = c_vc.count
                    E("vector", lambda e: e.tensor_scalar(tqb[:, :], tqb[:, :], 0.0, None,
                                                          Alu.max))
                    E("vector", lambda e, i=sidx: e.tensor_copy(
                        x[:, i * TT:(i + 1) * TT], tqb[:, :]), inc=c_vc)
            state["x_done"] = c_vc.count
        elif st_ == "tables":
            W("vector", c_vc, qtables_done)
            for i, src in enumerate((b1t, K1t, K2t, biasC, biasS)):
                E("vector", lambda e, i=i, src=src: e.tensor_copy(
                    x[:, i * 32:(i + 1) * 32], src), inc=c_vc)
            E("vector", lambda e: e.tensor_copy(x[:, 160:161], absA), inc=c_vc)
            E("vector", lambda e: e.tensor_copy(x[:, 161:162], phi2p), inc=c_vc)
            E("vector", lambda e: e.tensor_copy(x[:, 162:163], aR), inc=c_vc)
            E("vector", lambda e: e.tensor_copy(x[:, 163:164], aI), inc=c_vc)
            state["x_done"] = c_vc.count

    W("sync", c_vc, state["x_done"])
    for dk in range(DK):
        E("sync", lambda e, dk=dk: e.dma_start(out=OUT[128 * dk:128 * (dk + 1), :],
                                               in_=x[:, dk * T:(dk + 1) * T]), inc=c_sd)
    W("sync", c_sd, c_sd.count)

    P.check()
    with es, nc.Block() as block:
        @block.tensor
        def _(e):
            for fn in P.prog["tensor"]:
                fn(e)

        @block.vector
        def _(e):
            for fn in P.prog["vector"]:
                fn(e)

        @block.scalar
        def _(e):
            for fn in P.prog["scalar"]:
                fn(e)

        @block.gpsimd
        def _(e):
            for fn in P.prog["gpsimd"]:
                fn(e)

        @block.sync
        def _(e):
            for fn in P.prog["sync"]:
                fn(e)
    return nc


def kernel(_stages=("ffn", "ssm", "ffn"), **inputs):
    gi = lambda k: np.asarray(inputs[k], dtype=np.float32)
    u = gi("L0_attn_u")
    lar = gi("L0_attn_log_A_real")
    lai = gi("L0_attn_log_A_imag")
    Bw = gi("L0_attn_B_w")
    Cw = gi("L0_attn_C_w")
    Dw = gi("L0_attn_D_w")
    W1 = gi("L0_ffn_W1")
    b1 = gi("L0_ffn_b1")
    th = gi("L0_ffn_quantum_phases")
    wr = gi("L0_ffn_quantum_weights_real")
    wi = gi("L0_ffn_quantum_weights_imag")
    W2 = gi("L0_ffn_W2")
    b2 = gi("L0_ffn_b2")

    C = np.ascontiguousarray
    shared = {
        "W1": C(W1),
        "W2": C(W2),
        "BwT": C(Bw.T),
        "CwT": C(Cw.T),
        "DwT": C(Dw.T),
        "b1t": C(b1.reshape(32, 128).T),
        "b2t": C(b2.reshape(8, 128).T),
        "tht": C(th.reshape(32, 128).T),
        "wrt": C(wr.reshape(32, 128).T),
        "wit": C(wi.reshape(32, 128).T),
        "laR": C(np.concatenate([lar, lar]).reshape(128, 1)),
        "laI": C(np.concatenate([lai, lai]).reshape(128, 1)),
    }
    in_maps = []
    for c in range(8):
        m = dict(shared)
        m["xT"] = C(u[c].T)
        in_maps.append(m)

    nc = build_nc(stages=_stages)
    res = run_bass_kernel_spmd(nc, in_maps, core_ids=list(range(8)))
    out = np.stack([np.asarray(r["out"]).T for r in res.results])
    return np.ascontiguousarray(out).astype(np.float32)


if __name__ == "__main__":
    import reference as _r
    inp = _r.setup_inputs()
    out = kernel(**{k: np.asarray(v) for k, v in inp.items()})
    ref = np.asarray(_r.reference(**inp))
    err = np.linalg.norm(out - ref) / np.linalg.norm(ref)
    print("rel_err:", err)


# version 21
# speedup vs baseline: 1.0119x; 1.0119x over previous
import math
import numpy as np
import concourse.bass as bass
import concourse.mybir as mybir
from concourse.bass_utils import run_bass_kernel_spmd
from contextlib import ExitStack

F32 = mybir.dt.float32
BF16 = mybir.dt.bfloat16
I32 = mybir.dt.int32
Alu = mybir.AluOpType
Act = mybir.ActivationFunctionType

D = 1024
F = 4096
T = 2048
DK = D // 128
FK = F // 128
TT = 512
NTT = T // TT
FC = 256
NFC = F // FC
EPS = 1e-5
TWO_PI = 2.0 * math.pi
MAGIC = 12582912.0

ENGINES = ("tensor", "vector", "scalar", "gpsimd", "sync")


class Ch:
    def __init__(self, nc, es, name, step):
        self.h = es.enter_context(nc.semaphore(name))
        self.step = step
        self.count = 0
        self.name = name
        self.max_waited = 0


class Prog:
    def __init__(self):
        self.prog = {e: [] for e in ENGINES}
        self.chans = []

    def wait(self, eng, ch, cnt=None):
        v = ch.count if cnt is None else cnt
        if v <= 0:
            return
        ch.max_waited = max(ch.max_waited, v)
        self.prog[eng].append(lambda e, s=ch.h, vv=v: e.wait_ge(s, vv))

    def do(self, eng, fn, inc=None):
        if inc is None:
            self.prog[eng].append(fn)
            return None
        inc.count += inc.step

        def wrapped(e, fn=fn, s=inc.h, st=inc.step):
            fn(e).then_inc(s, st)
        self.prog[eng].append(wrapped)
        return inc.count

    def check(self):
        for c in self.chans:
            assert c.max_waited <= c.count, (
                f"channel {c.name}: waited {c.max_waited} > produced {c.count}")


def build_nc(stages=("ffn", "ssm", "ffn")):
    nc = bass.Bass("TRN2", target_bir_lowering=False)
    es = ExitStack()

    din = lambda n, s: nc.dram_tensor(n, s, F32, kind="ExternalInput").ap()
    xT = din("xT", [D, T])
    W1 = din("W1", [D, F])
    W2 = din("W2", [F, D])
    BwT = din("BwT", [D, 128])
    CwT = din("CwT", [128, D])
    DwT = din("DwT", [D, D])
    b1t_in = din("b1t", [128, 32])
    b2t_in = din("b2t", [128, 8])
    tht_in = din("tht", [128, 32])
    wrt_in = din("wrt", [128, 32])
    wit_in = din("wit", [128, 32])
    laR_in = din("laR", [128, 1])
    laI_in = din("laI", [128, 1])
    OUT = nc.dram_tensor("out", [D, T], F32, kind="ExternalOutput").ap()
    dscr = lambda n, s: nc.dram_tensor(n, s, BF16, kind="Internal").ap()
    W1S = [dscr("W1S_hi", [NFC, 128, DK * FC]), dscr("W1S_lo", [NFC, 128, DK * FC])]
    W2S = [dscr("W2S_hi", [DK, 128, F]), dscr("W2S_lo", [DK, 128, F])]
    DS = [dscr("DS_hi", [DK, 128, D]), dscr("DS_lo", [DK, 128, D])]

    sb = lambda n, s, d=F32: es.enter_context(nc.sbuf_tensor(n, s, d)).ap()
    x = sb("x", [128, DK * T])
    ws = sb("ws", [128, 24576])
    xnhi = sb("xnhi", [128, DK * TT], BF16)
    xnlo = sb("xnlo", [128, DK * TT], BF16)
    bwt_h = sb("bwt_h", [128, DK * 128], BF16)
    bwt_l = sb("bwt_l", [128, DK * 128], BF16)
    cwt_h = sb("cwt_h", [128, D], BF16)
    cwt_l = sb("cwt_l", [128, D], BF16)
    stat = sb("stat", [128, 1024])
    x2t = sb("x2t", [128, 1024])
    csl = sb("csl", [128, 1024])
    ssl = sb("ssl", [128, 1024])
    tqa = sb("tqa", [128, 512])
    tqb = sb("tqb", [128, 512])
    tbl = sb("tbl", [128, 256])
    one128 = sb("one128", [128, 1])
    one1 = sb("one1", [1, 128])

    b1t = tbl[:, 0:32]
    K1t = tbl[:, 32:64]
    K2t = tbl[:, 64:96]
    biasC = tbl[:, 96:128]
    biasS = tbl[:, 128:160]
    b2h = tbl[:, 160:168]
    aR = tbl[:, 168:169]; aI = tbl[:, 169:170]; absA = tbl[:, 170:171]
    phi2p = tbl[:, 171:172]
    t0_ = tbl[:, 172:173]; t1_ = tbl[:, 173:174]; t2_ = tbl[:, 174:175]
    t3_ = tbl[:, 175:176]; t4_ = tbl[:, 176:177]
    tht = tbl[:, 177:209]
    wrt = tbl[:, 209:241]
    wit = csl[:, 0:32]
    kqt = csl[:, 32:64]
    sth = csl[:, 64:96]
    cth = csl[:, 96:128]
    tg1 = ssl[:, 0:32]
    tg2 = ssl[:, 32:64]
    tg3 = ssl[:, 64:96]

    wsf = lambda a, b: ws[:, a:b]
    wsb = lambda a, b: ws[:, a:b].bitcast(BF16)
    qhi = wsb(0, 8192)
    qlo = wsb(8192, 16384)
    w1b = [wsb(16384 + 1024 * i, 17408 + 1024 * i) for i in range(4)]
    w2b = [wsb(20480 + 1024 * i, 21504 + 1024 * i) for i in range(4)]
    xn2hi, xn2lo = qhi, qlo
    Bu = wsf(16384, 18432)
    u_ = wsf(18432, 20480)
    g_ = wsf(20480, 22528)
    gX = wsf(22528, 24576)
    BuX = g_
    rotmp = u_
    h_ = Bu
    hhi = wsb(18432, 19456)
    hlo = wsb(19456, 20480)
    dd = [wsb(20480, 21504), wsb(21504, 22528)]
    cs_c = xnhi.bitcast(F32)
    cs_s = xnlo.bitcast(F32)
    stg = [wsf(0, 4096), wsf(4096, 8192)]
    ohi = [wsb(16384, 18432), wsb(20480, 22528)]
    olo = [wsb(18432, 20480), wsb(22528, 24576)]

    psv = [es.enter_context(nc.psum_tensor(f"p{i}", [128, 512], F32)).ap() for i in range(8)]

    P = Prog()
    ch = lambda n, s: P.chans.append(Ch(nc, es, n, s)) or P.chans[-1]
    c_sd = ch("c_sd", 16)
    c_gd = ch("c_gd", 16)
    c_tn = ch("c_tn", 1)
    c_vc = ch("c_vc", 1)
    c_ac = ch("c_ac", 1)
    c_xd = ch("c_xd", 16)
    W, E = P.wait, P.do

    bank_free = {i: 0 for i in range(8)}

    x_loaded_t = []
    for t in range(NTT):
        for dk in range(DK):
            E("scalar", lambda e, dk=dk, t=t: e.dma_start(
                out=x[:, dk * T + t * TT:dk * T + (t + 1) * TT],
                in_=xT[128 * dk:128 * (dk + 1), t * TT:(t + 1) * TT]), inc=c_xd)
        x_loaded_t.append(c_xd.count)
    x_loaded = c_xd.count
    for (dst, src) in ((b1t, b1t_in), (b2h, b2t_in), (tht, tht_in), (wrt, wrt_in),
                       (wit, wit_in), (aR, laR_in), (aI, laI_in)):
        E("sync", lambda e, d=dst, s=src: e.dma_start(out=d, in_=s[:, :]), inc=c_sd)
    small_loaded = c_sd.count

    E("vector", lambda e: e.memset(one128[:, :], 1.0))
    E("vector", lambda e: e.memset(one1[:, :], 1.0), inc=c_vc)
    ones_ready = c_vc.count

    W("scalar", c_sd, small_loaded)
    E("scalar", lambda e: e.activation(t0_, aR, Act.Exp), inc=c_ac)
    W("vector", c_ac)
    E("vector", lambda e: e.tensor_scalar(t0_, t0_, -0.5, None, Alu.mult))
    E("vector", lambda e: e.drain())
    E("vector", lambda e: e.tensor_scalar(t1_, aI, 0.5, None, Alu.mult))
    E("vector", lambda e: e.drain())
    E("vector", lambda e: e.tensor_scalar(t2_, t0_, -1.0, 1.0, Alu.mult, Alu.add))
    E("vector", lambda e: e.drain())
    E("vector", lambda e: e.tensor_tensor(t2_, t2_, t2_, Alu.mult))
    E("vector", lambda e: e.drain())
    E("vector", lambda e: e.tensor_tensor(t3_, t1_, t1_, Alu.mult))
    E("vector", lambda e: e.drain())
    E("vector", lambda e: e.tensor_tensor(t2_, t2_, t3_, Alu.add))
    E("vector", lambda e: e.drain())
    E("vector", lambda e: e.reciprocal(t2_, t2_))
    E("vector", lambda e: e.drain())
    E("vector", lambda e: e.tensor_tensor(t4_, t0_, t0_, Alu.mult))
    E("vector", lambda e: e.drain())
    E("vector", lambda e: e.tensor_tensor(t4_, t4_, t3_, Alu.add))
    E("vector", lambda e: e.drain())
    E("vector", lambda e: e.tensor_scalar(t4_, t4_, -1.0, 1.0, Alu.mult, Alu.add))
    E("vector", lambda e: e.drain())
    E("vector", lambda e: e.tensor_tensor(aR, t4_, t2_, Alu.mult))
    E("vector", lambda e: e.drain())
    E("vector", lambda e: e.tensor_tensor(t1_, t1_, t2_, Alu.mult))
    E("vector", lambda e: e.drain())
    E("vector", lambda e: e.tensor_scalar(aI, t1_, 2.0, None, Alu.mult))
    E("vector", lambda e: e.drain())
    E("vector", lambda e: e.tensor_tensor(t0_, aR, aR, Alu.mult))
    E("vector", lambda e: e.drain())
    E("vector", lambda e: e.tensor_tensor(t1_, aI, aI, Alu.mult))
    E("vector", lambda e: e.drain())
    E("vector", lambda e: e.tensor_tensor(t0_, t0_, t1_, Alu.add), inc=c_vc)
    E("vector", lambda e: e.drain())
    W("scalar", c_vc)
    E("scalar", lambda e: e.activation(absA, t0_, Act.Sqrt), inc=c_ac)
    W("vector", c_ac)
    E("vector", lambda e: e.reciprocal(t0_, aR))
    E("vector", lambda e: e.drain())
    E("vector", lambda e: e.tensor_tensor(t1_, aI, t0_, Alu.mult), inc=c_vc)
    E("vector", lambda e: e.drain())
    W("scalar", c_vc)
    E("scalar", lambda e: e.activation(t2_, t1_, Act.Arctan), inc=c_ac)
    E("scalar", lambda e: e.activation(t3_, aI, Act.Sign), inc=c_ac)
    W("vector", c_ac)
    E("vector", lambda e: e.tensor_scalar(t4_, aR, 0.0, None, Alu.is_lt))
    E("vector", lambda e: e.drain())
    E("vector", lambda e: e.tensor_tensor(t4_, t4_, t3_, Alu.mult))
    E("vector", lambda e: e.drain())
    E("vector", lambda e: e.scalar_tensor_tensor(t2_, t4_, math.pi, t2_, Alu.mult, Alu.add))
    E("vector", lambda e: e.drain())
    E("vector", lambda e: e.tensor_scalar(phi2p, t2_, 1.0 / TWO_PI, None, Alu.mult), inc=c_vc)
    E("vector", lambda e: e.drain())

    def round_reduce(dst, srcf):
        tgi = tg1.bitcast(I32)
        E("vector", lambda e: e.tensor_scalar(tg2, srcf, 1.0, 16384.0, Alu.mult, Alu.add))
        E("vector", lambda e: e.drain())
        E("vector", lambda e: e.tensor_copy(tgi, tg2))
        E("vector", lambda e: e.drain())
        E("vector", lambda e: e.tensor_copy(tg2, tgi))
        E("vector", lambda e: e.drain())
        E("vector", lambda e: e.tensor_scalar(tg2, tg2, 1.0, -16384.0, Alu.mult, Alu.add))
        E("vector", lambda e: e.drain())
        E("vector", lambda e: e.scalar_tensor_tensor(tg2, tg2, -1.0, srcf,
                                                     Alu.mult, Alu.add))
        E("vector", lambda e: e.drain())
        E("vector", lambda e: e.tensor_scalar(dst, tg2, TWO_PI, None, Alu.mult))
        E("vector", lambda e: e.drain())

    W("vector", c_sd, small_loaded)
    E("vector", lambda e: e.tensor_scalar(tg3, tht, 1.0 / TWO_PI, None, Alu.mult))
    E("vector", lambda e: e.drain())
    round_reduce(sth, tg3)
    E("vector", lambda e: e.tensor_scalar(tg3, tg3, 1.0, 0.25, Alu.mult, Alu.add))
    E("vector", lambda e: e.drain())
    round_reduce(cth, tg3)
    E("vector", lambda e: e.nop(), inc=c_vc)
    E("vector", lambda e: e.drain())
    W("scalar", c_vc)
    E("scalar", lambda e: e.activation(sth, sth, Act.Sin))
    E("scalar", lambda e: e.activation(cth, cth, Act.Sin), inc=c_ac)
    W("vector", c_ac)
    E("vector", lambda e: e.tensor_tensor(tg1, cth, wrt, Alu.mult))
    E("vector", lambda e: e.drain())
    E("vector", lambda e: e.tensor_tensor(tg2, sth, wit, Alu.mult))
    E("vector", lambda e: e.drain())
    E("vector", lambda e: e.tensor_tensor(tg1, tg1, tg2, Alu.add))
    E("vector", lambda e: e.drain())
    E("vector", lambda e: e.tensor_tensor(tg1, tg1, wit, Alu.mult))
    E("vector", lambda e: e.drain())
    E("vector", lambda e: e.tensor_scalar(kqt, tg1, 0.1, None, Alu.mult))
    E("vector", lambda e: e.drain())
    E("vector", lambda e: e.tensor_tensor(K1t, kqt, sth, Alu.mult))
    E("vector", lambda e: e.drain())
    E("vector", lambda e: e.tensor_tensor(K2t, kqt, cth, Alu.mult))
    E("vector", lambda e: e.drain())
    E("vector", lambda e: e.tensor_scalar(biasC, b1t, 0.1, math.pi / 2, Alu.mult, Alu.add))
    E("vector", lambda e: e.drain())
    E("vector", lambda e: e.tensor_scalar(biasS, b1t, 0.1, None, Alu.mult))
    E("vector", lambda e: e.drain())
    E("vector", lambda e: e.tensor_scalar(b2h, b2h, 0.5, None, Alu.mult), inc=c_vc)
    E("vector", lambda e: e.drain())
    qtables_done = c_vc.count

    units = []
    for fc in range(NFC):
        src = W1[:, fc * FC:(fc + 1) * FC].rearrange("(dk p) f -> p dk f", p=128)
        units.append(("w1", fc, src, DK * FC, DK))
    for dk in range(DK):
        src = W2[:, dk * 128:(dk + 1) * 128].rearrange("(fs p) d -> p fs d", p=128)
        units.append(("w2", dk, src, F, FK))
    units.append(("bwt", 0, BwT[:, :].rearrange("(dk p) n -> p dk n", p=128), DK * 128, DK))
    units.append(("cwt", 0, CwT[:, :].rearrange("n (a d2) -> n a d2", a=1), D, 1))
    for ek in range(DK):
        src = DwT[:, ek * 128:(ek + 1) * 128].rearrange("(dk p) e2 -> p dk e2", p=128)
        units.append(("dwt", ek, src, D, DK))

    vec_split = {}
    wb_done = {}
    for i, (kind, idx, src, n, mid) in enumerate(units):
        s = i % 2
        if i >= 2:
            W("sync", c_vc, vec_split[i - 2])
        E("sync", lambda e, src=src, s=s, n=n, mid=mid: e.dma_start(
            out=stg[s][:, 0:n].rearrange("p (a b) -> p a b", a=mid), in_=src),
          inc=c_sd)
        ld = c_sd.count
        W("vector", c_sd, ld)
        if i >= 2 and units[i - 2][0] not in ("bwt", "cwt"):
            W("vector", c_gd, wb_done[i - 2])
        if kind == "bwt":
            dsth, dstl = bwt_h[:, :], bwt_l[:, :]
        elif kind == "cwt":
            dsth, dstl = cwt_h[:, :], cwt_l[:, :]
        else:
            dsth, dstl = ohi[s][:, 0:n], olo[s][:, 0:n]
        E("vector", lambda e, d=dsth, s=s, n=n: e.tensor_copy(d, stg[s][:, 0:n]))
        E("vector", lambda e, dh=dsth, dl=dstl, s=s, n=n:
          e.scalar_tensor_tensor(dl, dh, -1.0, stg[s][:, 0:n], Alu.mult, Alu.add),
          inc=c_vc)
        vec_split[i] = c_vc.count
        if kind not in ("bwt", "cwt"):
            W("gpsimd", c_vc, vec_split[i])
            scr = {"w1": W1S, "w2": W2S, "dwt": DS}[kind]
            outs = (scr[0][idx], scr[1][idx])
            E("gpsimd", lambda e, o=outs[0], s=s, n=n: e.dma_start(out=o[:, :],
                                                                   in_=ohi[s][:, 0:n]), inc=c_gd)
            E("gpsimd", lambda e, o=outs[1], s=s, n=n: e.dma_start(out=o[:, :],
                                                                   in_=olo[s][:, 0:n]), inc=c_gd)
            wb_done[i] = c_gd.count
    w1_ready = [wb_done[i] for i, un in enumerate(units) if un[0] == "w1"]
    w2_ready = [wb_done[i] for i, un in enumerate(units) if un[0] == "w2"]
    d_ready = [wb_done[i] for i, un in enumerate(units) if un[0] == "dwt"]
    bwt_ready = [vec_split[i] for i, un in enumerate(units) if un[0] == "bwt"][0]
    cwt_ready = [vec_split[i] for i, un in enumerate(units) if un[0] == "cwt"][0]
    startup_sd_all = c_sd.count
    startup_gd_all = c_gd.count

    w1slot_free = [0, 0]
    w2slot_free = [0, 0]
    ddslot_free = [0, 0]
    sqslot_free = [0, 0]
    csslot_free = [0, 0]
    state = {"last_split_wb": c_gd.count}

    def ln_stats(t, wait_scalar_x, sq_vector=False):
        ts_ = t * TT
        mslot = stat[0:1, 0:512]
        rslot = stat[0:1, 512:1024]
        tmp1 = tqa[0:1, 0:512]
        W("tensor", c_xd, x_loaded_t[t])
        W("tensor", c_vc, max(bank_free[6], bank_free[7]))
        for dk in range(DK):
            s = dk % 2
            sqe = "vector" if sq_vector else "scalar"
            W(sqe, c_tn, sqslot_free[s])
            W(sqe, c_vc, wait_scalar_x)
            W(sqe, c_xd, x_loaded_t[t])
            if sq_vector:
                E("vector", lambda e, dk=dk, s=s: e.tensor_mul(
                    x2t[:, s * 512:(s + 1) * 512], x[:, dk * T + ts_:dk * T + ts_ + TT],
                    x[:, dk * T + ts_:dk * T + ts_ + TT]), inc=c_vc)
                sqch, sqv = c_vc, c_vc.count
            else:
                E("scalar", lambda e, dk=dk, s=s: e.activation(
                    x2t[:, s * 512:(s + 1) * 512], x[:, dk * T + ts_:dk * T + ts_ + TT],
                    Act.Square), inc=c_ac)
                sqch, sqv = c_ac, c_ac.count
            E("tensor", lambda e, dk=dk: e.matmul(
                psv[6][0:1, :], one128[:, :], x[:, dk * T + ts_:dk * T + ts_ + TT],
                start=(dk == 0), stop=(dk == DK - 1), skip_group_check=True))
            W("tensor", sqch, sqv)
            E("tensor", lambda e, s=s, dk=dk: e.matmul(
                psv[7][0:1, :], one128[:, :], x2t[:, s * 512:(s + 1) * 512],
                start=(dk == 0), stop=(dk == DK - 1), skip_group_check=True), inc=c_tn)
            sqslot_free[s] = c_tn.count
        stats_mm = c_tn.count
        W("vector", c_tn, stats_mm)
        E("vector", lambda e, m=mslot: e.tensor_scalar(m, psv[6][0:1, :], 1.0 / D, None, Alu.mult))
        E("vector", lambda e, r=rslot: e.tensor_scalar(r, psv[7][0:1, :], 1.0 / D, None, Alu.mult))
        E("vector", lambda e, m=mslot: e.tensor_tensor(tmp1, m, m, Alu.mult))
        E("vector", lambda e, r=rslot: e.tensor_tensor(r, r, tmp1, Alu.subtract))
        E("vector", lambda e, r=rslot: e.tensor_scalar(r, r, EPS, None, Alu.add), inc=c_vc)
        W("scalar", c_vc)
        E("scalar", lambda e, r=rslot: e.activation(r, r, Act.Sqrt), inc=c_ac)
        W("vector", c_ac)
        E("vector", lambda e, r=rslot: e.reciprocal(r, r), inc=c_vc)
        return c_vc.count

    def ln_apply(t, r_done, xn_hi_dk, xn_lo_dk, wait_vec_xnwrite):
        ts_ = t * TT
        mslot = stat[0:1, 0:512]
        rslot = stat[0:1, 512:1024]
        W("tensor", c_vc, r_done)
        E("tensor", lambda e, m=mslot: e.matmul(psv[6][:, :], one1[:, :], m,
                                                start=True, stop=True, skip_group_check=True))
        E("tensor", lambda e, r=rslot: e.matmul(psv[7][:, :], one1[:, :], r,
                                                start=True, stop=True, skip_group_check=True),
          inc=c_tn)
        bcast_done = c_tn.count
        W("vector", c_tn, bcast_done)
        W("vector", c_tn, wait_vec_xnwrite)
        for dk in range(DK):
            E("vector", lambda e, dk=dk: e.tensor_tensor(
                tqa[:, :], x[:, dk * T + ts_:dk * T + ts_ + TT], psv[6][:, :], Alu.subtract))
            E("vector", lambda e: e.tensor_tensor(tqa[:, :], tqa[:, :], psv[7][:, :], Alu.mult))
            E("vector", lambda e, dk=dk: e.tensor_copy(xn_hi_dk(dk), tqa[:, :]))
            E("vector", lambda e, dk=dk: e.scalar_tensor_tensor(
                xn_lo_dk(dk), xn_hi_dk(dk), -1.0, tqa[:, :], Alu.mult, Alu.add), inc=c_vc)
        bank_free[6] = c_vc.count
        bank_free[7] = c_vc.count
        return c_vc.count

    def emit_ln(t, xn_hi_dk, xn_lo_dk, wait_vec_xnwrite, wait_scalar_x):
        r_done = ln_stats(t, wait_scalar_x)
        return ln_apply(t, r_done, xn_hi_dk, xn_lo_dk, wait_vec_xnwrite)

    def emit_ffn(tables_hook=None):
        first_call = state["first_ffn"]
        state["first_ffn"] = False
        xn_hi_dk = lambda dk: xnhi[:, dk * TT:(dk + 1) * TT]
        xn_lo_dk = lambda dk: xnlo[:, dk * TT:(dk + 1) * TT]
        if state.get("pre_xn0") is not None:
            xn_ready = state.pop("pre_xn0")
        else:
            xn_ready = emit_ln(0, xn_hi_dk, xn_lo_dk,
                               state["xn_reader"], state["x_done"])
        if first_call:
            W("sync", c_gd, startup_gd_all)
            W("vector", c_sd, startup_sd_all)
            W("vector", c_gd, startup_gd_all)
        W("sync", c_tn, state["xn_reader"])
        for t in range(NTT):
            ts_ = t * TT
            a_first_group = None
            for fc in range(NFC):
                s = fc % 2
                W("sync", c_tn, w1slot_free[s])
                W("sync", c_gd, w1_ready[fc])
                W("sync", c_vc, xn_ready)
                E("sync", lambda e, fc=fc, s=s: e.dma_start(out=w1b[2 * s][:, :],
                                                            in_=W1S[0][fc][:, :]), inc=c_sd)
                E("sync", lambda e, fc=fc, s=s: e.dma_start(out=w1b[2 * s + 1][:, :],
                                                            in_=W1S[1][fc][:, :]), inc=c_sd)
                w1_ld = c_sd.count
                for fs in range(2):
                    sidx = fc * 2 + fs
                    bank = 2 + (sidx % 4)
                    cslot = sidx % 2
                    W("tensor", c_sd, w1_ld)
                    W("tensor", c_vc, xn_ready)
                    W("tensor", c_vc, bank_free[bank])
                    for p in range(3):
                        lh = w1b[2 * s] if p < 2 else w1b[2 * s + 1]
                        rh = xn_hi_dk if p != 1 else xn_lo_dk
                        for dk in range(DK):
                            lhs = lh[:, dk * FC + fs * 128:dk * FC + fs * 128 + 128]
                            first = (p == 0 and dk == 0)
                            last = (p == 2 and dk == DK - 1)
                            E("tensor", lambda e, lhs=lhs, r=rh(dk), b=bank, f=first, l=last:
                              e.matmul(psv[b][:, :], lhs, r, start=f, stop=l,
                                       skip_group_check=True),
                              inc=(c_tn if last else None))
                    mm_done = c_tn.count
                    if a_first_group is None:
                        a_first_group = mm_done
                    w1slot_free[s] = mm_done
                    W("scalar", c_tn, mm_done)
                    W("scalar", c_vc, csslot_free[cslot])
                    E("scalar", lambda e, b=bank, c2=cslot, i=sidx: e.activation(
                        csl[:, c2 * 512:(c2 + 1) * 512], psv[b][:, :], Act.Sin,
                        bias=biasC[:, i:i + 1], scale=0.1))
                    E("scalar", lambda e, b=bank, c2=cslot, i=sidx: e.activation(
                        ssl[:, c2 * 512:(c2 + 1) * 512], psv[b][:, :], Act.Sin,
                        bias=biasS[:, i:i + 1], scale=0.1), inc=c_ac)
                    sins_done = c_ac.count
                    W("vector", c_ac, sins_done)
                    cap = csl[:, cslot * 512:(cslot + 1) * 512]
                    sap = ssl[:, cslot * 512:(cslot + 1) * 512]
                    E("vector", lambda e, cap=cap, i=sidx: e.tensor_scalar(
                        tqa[:, :], cap, K1t[:, i:i + 1], b1t[:, i:i + 1], Alu.mult, Alu.add))
                    E("vector", lambda e, sap=sap, i=sidx: e.scalar_tensor_tensor(
                        tqa[:, :], sap, K2t[:, i:i + 1], tqa[:, :], Alu.mult, Alu.add),
                      inc=c_vc)
                    csslot_free[cslot] = c_vc.count
                    E("vector", lambda e, b=bank: e.tensor_tensor(
                        tqb[:, :], tqa[:, :], psv[b][:, :], Alu.add), inc=c_vc)
                    bank_free[bank] = c_vc.count
                    E("vector", lambda e: e.tensor_scalar(tqb[:, :], tqb[:, :], 0.0, None,
                                                          Alu.max))
                    qh = qhi[:, sidx * TT:(sidx + 1) * TT]
                    ql = qlo[:, sidx * TT:(sidx + 1) * TT]
                    E("vector", lambda e, qh=qh: e.tensor_copy(qh, tqb[:, :]))
                    E("vector", lambda e, qh=qh, ql=ql: e.scalar_tensor_tensor(
                        ql, qh, -1.0, tqb[:, :], Alu.mult, Alu.add), inc=c_vc)
            q_ready = c_vc.count
            a_last_mm = c_tn.count
            if t < NTT - 1:
                r_done_next = ln_stats(t + 1, state["x_done"])
            for dk in range(DK):
                bank = dk % 6
                if t < NTT - 1 and dk == 1:
                    xn_ready = ln_apply(t + 1, r_done_next, xn_hi_dk, xn_lo_dk, a_last_mm)
                if t == NTT - 1 and dk == 3 and tables_hook is not None:
                    tables_hook(a_last_mm)
                W("tensor", c_vc, bank_free[bank])
                for h in range(2):
                    un = dk * 2 + h
                    bp = un % 2
                    W("sync", c_tn, w2slot_free[bp])
                    W("sync", c_gd, w2_ready[dk])
                    W("sync", c_vc, state["w2region_guard"])
                    E("sync", lambda e, dk=dk, h=h, bp=bp: e.dma_start(
                        out=w2b[2 * bp][:, :], in_=W2S[0][dk][:, h * 2048:(h + 1) * 2048]),
                      inc=c_sd)
                    E("sync", lambda e, dk=dk, h=h, bp=bp: e.dma_start(
                        out=w2b[2 * bp + 1][:, :], in_=W2S[1][dk][:, h * 2048:(h + 1) * 2048]),
                      inc=c_sd)
                    w2_ld = c_sd.count
                    W("tensor", c_sd, w2_ld)
                    W("tensor", c_vc, q_ready)
                    for fs in range(h * 16, h * 16 + 16):
                        o = (fs - h * 16) * 128
                        for p in range(3):
                            lhs = (w2b[2 * bp] if p < 2 else w2b[2 * bp + 1])[:, o:o + 128]
                            rh = (qhi if p != 1 else qlo)[:, fs * TT:(fs + 1) * TT]
                            first = (h == 0 and fs == 0 and p == 0)
                            stop = (h == 1 and fs == h * 16 + 15 and p == 2)
                            unit_end = (fs == h * 16 + 15 and p == 2)
                            E("tensor", lambda e, lhs=lhs, rh=rh, b=bank, f=first, l=stop:
                              e.matmul(psv[b][:, :], lhs, rh, start=f, stop=l,
                                       skip_group_check=True),
                              inc=(c_tn if unit_end else None))
                    w2slot_free[bp] = c_tn.count
                b_done = c_tn.count
                W("vector", c_tn, b_done)
                E("vector", lambda e, b=bank, dk=dk: e.tensor_scalar(
                    tqb[:, :], psv[b][:, :], 0.5, b2h[:, dk:dk + 1], Alu.mult, Alu.add))
                E("vector", lambda e, dk=dk, ts_=ts_: e.tensor_tensor(
                    x[:, dk * T + ts_:dk * T + ts_ + TT],
                    x[:, dk * T + ts_:dk * T + ts_ + TT], tqb[:, :], Alu.add), inc=c_vc)
                bank_free[bank] = c_vc.count
        state["x_done"] = c_vc.count
        state["xn_reader"] = c_tn.count
        state["w2region_guard"] = 0

    def emit_scan_tables(wait_tn):
        tmp1 = wsf(16384, 18432)
        tmp2 = wsf(18432, 20480)
        W("gpsimd", c_tn, wait_tn)
        E("gpsimd", lambda e: e.iota(tmp1[:, :], [[1, T]], base=0, channel_multiplier=0,
                                     allow_small_or_imprecise_dtypes=True), inc=c_gd)
        iota_done = c_gd.count
        W("vector", c_gd, iota_done)
        W("vector", c_tn, wait_tn)
        csi = xnhi.bitcast(I32)
        E("vector", lambda e: e.tensor_scalar(tmp2[:, :], tmp1[:, :], phi2p[:, 0:1], None,
                                              Alu.mult))
        E("vector", lambda e: e.tensor_scalar(tmp1[:, :], tmp2[:, :], 1.0, 16384.0,
                                              Alu.mult, Alu.add))
        E("vector", lambda e: e.tensor_copy(csi[:, :], tmp1[:, :]))
        E("vector", lambda e: e.tensor_copy(tmp1[:, :], csi[:, :]))
        E("vector", lambda e: e.tensor_scalar(tmp1[:, :], tmp1[:, :], 1.0, -16384.0,
                                              Alu.mult, Alu.add))
        E("vector", lambda e: e.scalar_tensor_tensor(tmp1[:, :], tmp1[:, :], -1.0,
                                                     tmp2[:, :], Alu.mult, Alu.add))
        E("vector", lambda e: e.tensor_scalar(cs_s[:, :], tmp1[:, :], TWO_PI, None,
                                              Alu.mult))
        E("vector", lambda e: e.tensor_scalar(tmp2[:, :], tmp2[:, :], 1.0, 0.25,
                                              Alu.mult, Alu.add))
        E("vector", lambda e: e.tensor_scalar(tmp1[:, :], tmp2[:, :], 1.0, 16384.0,
                                              Alu.mult, Alu.add))
        E("vector", lambda e: e.tensor_copy(csi[:, :], tmp1[:, :]))
        E("vector", lambda e: e.tensor_copy(tmp1[:, :], csi[:, :]))
        E("vector", lambda e: e.tensor_scalar(tmp1[:, :], tmp1[:, :], 1.0, -16384.0,
                                              Alu.mult, Alu.add))
        E("vector", lambda e: e.scalar_tensor_tensor(tmp1[:, :], tmp1[:, :], -1.0,
                                                     tmp2[:, :], Alu.mult, Alu.add))
        E("vector", lambda e: e.tensor_scalar(cs_c[:, :], tmp1[:, :], TWO_PI, None,
                                              Alu.mult),
          inc=c_vc)
        args_ready = c_vc.count
        W("scalar", c_vc, args_ready)
        E("scalar", lambda e: e.activation(cs_s[:, :], cs_s[:, :], Act.Sin))
        E("scalar", lambda e: e.activation(cs_c[:, :], cs_c[:, :], Act.Sin), inc=c_ac)
        state["cs_done"] = c_ac.count
        state["tables_emitted"] = True

    def emit_ssm(dump=False):
        ssm_start_vc = c_vc.count
        xn2_hi_dk = lambda dk, ts_: xn2hi[:, dk * T + ts_:dk * T + ts_ + TT]
        xn2_lo_dk = lambda dk, ts_: xn2lo[:, dk * T + ts_:dk * T + ts_ + TT]

        if not state.get("tables_emitted"):
            emit_scan_tables(state["xn_reader"])
        cs_done = state["cs_done"]

        def bproj(t, xn2_rdy):
            ts_ = t * TT
            bank = t % 2
            W("tensor", c_vc, bank_free[bank])
            W("tensor", c_vc, bwt_ready)
            W("tensor", c_vc, xn2_rdy)
            for p in range(3):
                lh = bwt_h if p < 2 else bwt_l
                rh = xn2_hi_dk if p != 1 else xn2_lo_dk
                for dk in range(DK):
                    first = (p == 0 and dk == 0)
                    last = (p == 2 and dk == DK - 1)
                    E("tensor", lambda e, l2=lh[:, dk * 128:(dk + 1) * 128],
                      r=rh(dk, ts_), b=bank, f=first, l=last:
                      e.matmul(psv[b][:, :], l2, r, start=f, stop=l, skip_group_check=True),
                      inc=(c_tn if last else None))
            bu_mm = c_tn.count
            W("vector", c_tn, bu_mm)
            E("vector", lambda e, b=bank, ts_=ts_: e.tensor_copy(
                Bu[:, ts_:ts_ + TT], psv[b][:, :]), inc=c_vc)
            bank_free[bank] = c_vc.count

        hi0 = lambda t: (lambda dk, ts_=t * TT: xn2_hi_dk(dk, ts_))
        lo0 = lambda t: (lambda dk, ts_=t * TT: xn2_lo_dk(dk, ts_))
        xn_rdy = {}
        r_d = ln_stats(0, state["x_done"], sq_vector=True)
        xn_rdy[0] = ln_apply(0, r_d, hi0(0), lo0(0), state["xn_reader"])
        r_next = ln_stats(1, state["x_done"], sq_vector=True)
        for t in range(NTT):
            bproj(t, xn_rdy[t])
            if t < NTT - 1:
                xn_rdy[t + 1] = ln_apply(t + 1, r_next, hi0(t + 1), lo0(t + 1),
                                         state["xn_reader"])
                if t < NTT - 2:
                    r_next = ln_stats(t + 2, state["x_done"], sq_vector=True)
        bu_all = c_vc.count

        if dump:
            E("vector", lambda e: e.tensor_copy(x[:, 2048:4096], Bu[:, :]), inc=c_vc)
        W("gpsimd", c_vc, bu_all)
        E("gpsimd", lambda e: e.dma_start(out=BuX[0:64, :], in_=Bu[64:128, :]), inc=c_gd)
        E("gpsimd", lambda e: e.dma_start(out=BuX[64:128, :], in_=Bu[0:64, :]), inc=c_gd)
        swap1 = c_gd.count
        W("vector", c_gd, swap1)
        W("vector", c_ac, cs_done)
        E("vector", lambda e: e.tensor_tensor(u_[:, :], Bu[:, :], cs_c[:, :], Alu.mult))
        E("vector", lambda e: e.tensor_tensor(gX[:, :], BuX[:, :], cs_s[:, :], Alu.mult))
        E("vector", lambda e: e.tensor_tensor(u_[0:64, :], u_[0:64, :], gX[0:64, :], Alu.add))
        E("vector", lambda e: e.tensor_tensor(u_[64:128, :], u_[64:128, :], gX[64:128, :],
                                              Alu.subtract))
        E("vector", lambda e: e.tensor_tensor_scan(
            g_[:, :], absA[:, 0:1].broadcast_to((128, T)), u_[:, :], 0.0,
            Alu.mult, Alu.add), inc=c_vc)
        scan_done = c_vc.count
        W("gpsimd", c_vc, scan_done)
        E("gpsimd", lambda e: e.dma_start(out=gX[0:64, :], in_=g_[64:128, :]), inc=c_gd)
        E("gpsimd", lambda e: e.dma_start(out=gX[64:128, :], in_=g_[0:64, :]), inc=c_gd)
        swap2 = c_gd.count
        W("vector", c_gd, swap2)
        E("vector", lambda e: e.tensor_tensor(rotmp[:, :], gX[:, :], cs_s[:, :], Alu.mult))
        E("vector", lambda e: e.tensor_tensor(h_[:, :], g_[:, :], cs_c[:, :], Alu.mult))
        E("vector", lambda e: e.tensor_tensor(h_[0:64, :], h_[0:64, :], rotmp[0:64, :],
                                              Alu.subtract))
        E("vector", lambda e: e.tensor_tensor(h_[64:128, :], h_[64:128, :], rotmp[64:128, :],
                                              Alu.add), inc=c_vc)
        rot_out_done = c_vc.count
        if dump:
            E("vector", lambda e: e.tensor_copy(x[:, 0:2048], h_[:, :]))
            E("vector", lambda e: e.tensor_copy(x[:, 4096:6144], cs_c[:, :]))
            E("vector", lambda e: e.tensor_copy(x[:, 6144:8192], cs_s[:, :]), inc=c_vc)
            state["x_done"] = c_vc.count
            state["xn_reader"] = c_tn.count
            state["w2region_guard"] = c_vc.count
            return
        E("vector", lambda e: e.tensor_copy(hhi[:, :], h_[:, :]))
        E("vector", lambda e: e.scalar_tensor_tensor(hlo[:, :], hhi[:, :], -1.0, h_[:, :],
                                                     Alu.mult, Alu.add), inc=c_vc)
        h_split = c_vc.count

        dd2 = [csl[:, :].bitcast(BF16), ssl[:, :].bitcast(BF16)]
        pairs = [(t, ek) for t in range(NTT) for ek in range(DK)]

        def dpart(i):
            t, ek = pairs[i]
            ts_ = t * TT
            sl = i % 2
            bank = i % 6
            W("sync", c_tn, ddslot_free[sl])
            W("sync", c_vc, ssm_start_vc)
            W("sync", c_gd, d_ready[ek])
            E("sync", lambda e, ek=ek, sl=sl: e.dma_start(
                out=dd2[sl][:, 0:1024], in_=DS[0][ek][:, :]), inc=c_sd)
            E("sync", lambda e, ek=ek, sl=sl: e.dma_start(
                out=dd2[sl][:, 1024:2048], in_=DS[1][ek][:, :]), inc=c_sd)
            d_ld = c_sd.count
            W("tensor", c_vc, bank_free[bank])
            W("tensor", c_sd, d_ld)
            for p in range(3):
                off = 0 if p < 2 else 1024
                rh = xn2_hi_dk if p != 1 else xn2_lo_dk
                for dk in range(DK):
                    E("tensor", lambda e, l2=dd2[sl][:, off + dk * 128:off + (dk + 1) * 128],
                      r=rh(dk, ts_), b=bank, f=(p == 0 and dk == 0):
                      e.matmul(psv[b][:, :], l2, r, start=f, stop=False,
                               skip_group_check=True),
                      inc=(c_tn if (p == 2 and dk == DK - 1) else None))
            ddslot_free[sl] = c_tn.count

        def cpart(i):
            t, ek = pairs[i]
            ts_ = t * TT
            bank = i % 6
            W("tensor", c_vc, max(h_split, cwt_ready))
            for p in range(3):
                lh = cwt_h if p < 2 else cwt_l
                rh2 = hhi if p != 1 else hlo
                E("tensor", lambda e, l2=lh[:, ek * 128:(ek + 1) * 128],
                  r=rh2[:, ts_:ts_ + TT], b=bank, l=(p == 2):
                  e.matmul(psv[b][:, :], l2, r, start=False, stop=l,
                           skip_group_check=True),
                  inc=(c_tn if p == 2 else None))
            cd_mm = c_tn.count
            W("vector", c_tn, cd_mm)
            E("vector", lambda e, b=bank, ek=ek, ts_=ts_: e.tensor_tensor(
                x[:, ek * T + ts_:ek * T + ts_ + TT],
                x[:, ek * T + ts_:ek * T + ts_ + TT], psv[b][:, :], Alu.add), inc=c_vc)
            bank_free[bank] = c_vc.count

        DEPTH = 2
        for i in range(min(DEPTH, len(pairs))):
            dpart(i)
        for i in range(len(pairs)):
            cpart(i)
            if i + DEPTH < len(pairs):
                dpart(i + DEPTH)
            if i == DK - 1:
                r0f = ln_stats(0, c_vc.count)
                state["pre_xn0"] = ln_apply(
                    0, r0f, lambda dk: xnhi[:, dk * TT:(dk + 1) * TT],
                    lambda dk: xnlo[:, dk * TT:(dk + 1) * TT], 0)
        state["x_done"] = c_vc.count
        state["xn_reader"] = c_tn.count
        state["w2region_guard"] = rot_out_done
    state["x_done"] = 0
    csslot_free[0] = qtables_done
    csslot_free[1] = qtables_done
    state["xn_reader"] = 0
    state["w2region_guard"] = 0
    state["first_ffn"] = True
    state["cs_done"] = 0
    state["tables_emitted"] = False
    if stages and stages[0] == "ffn":
        _r0 = ln_stats(0, 0)
        state["pre_xn0"] = ln_apply(0, _r0, lambda dk: xnhi[:, dk * TT:(dk + 1) * TT],
                                    lambda dk: xnlo[:, dk * TT:(dk + 1) * TT], 0)
    first_stage = True
    for st_ in stages:
        if st_ == "ffn":
            hook = emit_scan_tables if (first_stage and "ssm" in stages) else None
            emit_ffn(tables_hook=hook)
            first_stage = False
        elif st_ == "ssm":
            emit_ssm()
        elif st_ == "ssmh":
            emit_ssm(dump=True)
        elif st_ == "ln":
            xn_hi_dk = lambda dk: xnhi[:, dk * TT:(dk + 1) * TT]
            xn_lo_dk = lambda dk: xnlo[:, dk * TT:(dk + 1) * TT]
            for t in range(NTT):
                ts_ = t * TT
                xr = emit_ln(t, xn_hi_dk, xn_lo_dk, state["xn_reader"], state["x_done"])
                for dk in range(DK):
                    E("vector", lambda e, dk=dk, ts_=ts_: e.tensor_tensor(
                        x[:, dk * T + ts_:dk * T + ts_ + TT], xn_hi_dk(dk), xn_lo_dk(dk),
                        Alu.add), inc=c_vc)
                state["xn_reader"] = c_tn.count
                state["x_done"] = c_vc.count
            state["x_done"] = c_vc.count
        elif st_ == "ffnq":
            xn_hi_dk = lambda dk: xnhi[:, dk * TT:(dk + 1) * TT]
            xn_lo_dk = lambda dk: xnlo[:, dk * TT:(dk + 1) * TT]
            xn_ready = emit_ln(0, xn_hi_dk, xn_lo_dk, state["xn_reader"], state["x_done"])
            W("sync", c_gd, startup_gd_all)
            W("vector", c_sd, startup_sd_all)
            W("vector", c_gd, startup_gd_all)
            for fc in range(NFC):
                s = fc % 2
                W("sync", c_tn, w1slot_free[s])
                W("sync", c_gd, w1_ready[fc])
                W("sync", c_vc, xn_ready)
                E("sync", lambda e, fc=fc, s=s: e.dma_start(out=w1b[2 * s][:, :],
                                                            in_=W1S[0][fc][:, :]), inc=c_sd)
                E("sync", lambda e, fc=fc, s=s: e.dma_start(out=w1b[2 * s + 1][:, :],
                                                            in_=W1S[1][fc][:, :]), inc=c_sd)
                w1_ld = c_sd.count
                for fs in range(2):
                    sidx = fc * 2 + fs
                    bank = 2 + (sidx % 4)
                    cslot = sidx % 2
                    W("tensor", c_sd, w1_ld)
                    W("tensor", c_vc, xn_ready)
                    W("tensor", c_vc, bank_free[bank])
                    for p in range(3):
                        lh = w1b[2 * s] if p < 2 else w1b[2 * s + 1]
                        rh = xn_hi_dk if p != 1 else xn_lo_dk
                        for dk in range(DK):
                            lhs = lh[:, dk * FC + fs * 128:dk * FC + fs * 128 + 128]
                            first = (p == 0 and dk == 0)
                            last = (p == 2 and dk == DK - 1)
                            E("tensor", lambda e, lhs=lhs, r=rh(dk), b=bank, f=first, l=last:
                              e.matmul(psv[b][:, :], lhs, r, start=f, stop=l,
                                       skip_group_check=True),
                              inc=(c_tn if last else None))
                    mm_done = c_tn.count
                    w1slot_free[s] = mm_done
                    W("scalar", c_tn, mm_done)
                    W("scalar", c_vc, csslot_free[cslot])
                    E("scalar", lambda e, b=bank, c2=cslot, i=sidx: e.activation(
                        csl[:, c2 * 512:(c2 + 1) * 512], psv[b][:, :], Act.Sin,
                        bias=biasC[:, i:i + 1], scale=0.1))
                    E("scalar", lambda e, b=bank, c2=cslot, i=sidx: e.activation(
                        ssl[:, c2 * 512:(c2 + 1) * 512], psv[b][:, :], Act.Sin,
                        bias=biasS[:, i:i + 1], scale=0.1), inc=c_ac)
                    sins_done = c_ac.count
                    W("vector", c_ac, sins_done)
                    cap = csl[:, cslot * 512:(cslot + 1) * 512]
                    sap = ssl[:, cslot * 512:(cslot + 1) * 512]
                    E("vector", lambda e, cap=cap, i=sidx: e.tensor_scalar(
                        tqa[:, :], cap, K1t[:, i:i + 1], b1t[:, i:i + 1], Alu.mult, Alu.add))
                    E("vector", lambda e, sap=sap, i=sidx: e.scalar_tensor_tensor(
                        tqa[:, :], sap, K2t[:, i:i + 1], tqa[:, :], Alu.mult, Alu.add),
                      inc=c_vc)
                    csslot_free[cslot] = c_vc.count
                    E("vector", lambda e, b=bank: e.tensor_tensor(
                        tqb[:, :], tqa[:, :], psv[b][:, :], Alu.add), inc=c_vc)
                    bank_free[bank] = c_vc.count
                    E("vector", lambda e: e.tensor_scalar(tqb[:, :], tqb[:, :], 0.0, None,
                                                          Alu.max))
                    E("vector", lambda e, i=sidx: e.tensor_copy(
                        x[:, i * TT:(i + 1) * TT], tqb[:, :]), inc=c_vc)
            state["x_done"] = c_vc.count
        elif st_ == "tables":
            W("vector", c_vc, qtables_done)
            for i, src in enumerate((b1t, K1t, K2t, biasC, biasS)):
                E("vector", lambda e, i=i, src=src: e.tensor_copy(
                    x[:, i * 32:(i + 1) * 32], src), inc=c_vc)
            E("vector", lambda e: e.tensor_copy(x[:, 160:161], absA), inc=c_vc)
            E("vector", lambda e: e.tensor_copy(x[:, 161:162], phi2p), inc=c_vc)
            E("vector", lambda e: e.tensor_copy(x[:, 162:163], aR), inc=c_vc)
            E("vector", lambda e: e.tensor_copy(x[:, 163:164], aI), inc=c_vc)
            state["x_done"] = c_vc.count

    W("sync", c_vc, state["x_done"])
    W("scalar", c_vc, state["x_done"])
    for dk in range(DK):
        eng, chn = ("sync", c_sd) if dk % 2 == 0 else ("scalar", c_xd)
        E(eng, lambda e, dk=dk: e.dma_start(out=OUT[128 * dk:128 * (dk + 1), :],
                                            in_=x[:, dk * T:(dk + 1) * T]), inc=chn)
    W("sync", c_sd, c_sd.count)
    W("scalar", c_xd, c_xd.count)

    P.check()
    with es, nc.Block() as block:
        @block.tensor
        def _(e):
            for fn in P.prog["tensor"]:
                fn(e)

        @block.vector
        def _(e):
            for fn in P.prog["vector"]:
                fn(e)

        @block.scalar
        def _(e):
            for fn in P.prog["scalar"]:
                fn(e)

        @block.gpsimd
        def _(e):
            for fn in P.prog["gpsimd"]:
                fn(e)

        @block.sync
        def _(e):
            for fn in P.prog["sync"]:
                fn(e)
    return nc


def kernel(_stages=("ffn", "ssm", "ffn"), **inputs):
    gi = lambda k: np.asarray(inputs[k], dtype=np.float32)
    u = gi("L0_attn_u")
    lar = gi("L0_attn_log_A_real")
    lai = gi("L0_attn_log_A_imag")
    Bw = gi("L0_attn_B_w")
    Cw = gi("L0_attn_C_w")
    Dw = gi("L0_attn_D_w")
    W1 = gi("L0_ffn_W1")
    b1 = gi("L0_ffn_b1")
    th = gi("L0_ffn_quantum_phases")
    wr = gi("L0_ffn_quantum_weights_real")
    wi = gi("L0_ffn_quantum_weights_imag")
    W2 = gi("L0_ffn_W2")
    b2 = gi("L0_ffn_b2")

    C = np.ascontiguousarray
    shared = {
        "W1": C(W1),
        "W2": C(W2),
        "BwT": C(Bw.T),
        "CwT": C(Cw.T),
        "DwT": C(Dw.T),
        "b1t": C(b1.reshape(32, 128).T),
        "b2t": C(b2.reshape(8, 128).T),
        "tht": C(th.reshape(32, 128).T),
        "wrt": C(wr.reshape(32, 128).T),
        "wit": C(wi.reshape(32, 128).T),
        "laR": C(np.concatenate([lar, lar]).reshape(128, 1)),
        "laI": C(np.concatenate([lai, lai]).reshape(128, 1)),
    }
    in_maps = []
    for c in range(8):
        m = dict(shared)
        m["xT"] = C(u[c].T)
        in_maps.append(m)

    nc = build_nc(stages=_stages)
    res = run_bass_kernel_spmd(nc, in_maps, core_ids=list(range(8)))
    out = np.stack([np.asarray(r["out"]).T for r in res.results])
    return np.ascontiguousarray(out).astype(np.float32)


if __name__ == "__main__":
    import reference as _r
    inp = _r.setup_inputs()
    out = kernel(**{k: np.asarray(v) for k, v in inp.items()})
    ref = np.asarray(_r.reference(**inp))
    err = np.linalg.norm(out - ref) / np.linalg.norm(ref)
    print("rel_err:", err)


# version 22
# speedup vs baseline: 1.0161x; 1.0041x over previous
import math
import numpy as np
import concourse.bass as bass
import concourse.mybir as mybir
from concourse.bass_utils import run_bass_kernel_spmd
from contextlib import ExitStack

F32 = mybir.dt.float32
BF16 = mybir.dt.bfloat16
I32 = mybir.dt.int32
Alu = mybir.AluOpType
Act = mybir.ActivationFunctionType

D = 1024
F = 4096
T = 2048
DK = D // 128
FK = F // 128
TT = 512
NTT = T // TT
FC = 256
NFC = F // FC
EPS = 1e-5
TWO_PI = 2.0 * math.pi
MAGIC = 12582912.0

ENGINES = ("tensor", "vector", "scalar", "gpsimd", "sync")


class Ch:
    def __init__(self, nc, es, name, step):
        self.h = es.enter_context(nc.semaphore(name))
        self.step = step
        self.count = 0
        self.name = name
        self.max_waited = 0


class Prog:
    def __init__(self):
        self.prog = {e: [] for e in ENGINES}
        self.chans = []

    def wait(self, eng, ch, cnt=None):
        v = ch.count if cnt is None else cnt
        if v <= 0:
            return
        ch.max_waited = max(ch.max_waited, v)
        self.prog[eng].append(lambda e, s=ch.h, vv=v: e.wait_ge(s, vv))

    def do(self, eng, fn, inc=None):
        if inc is None:
            self.prog[eng].append(fn)
            return None
        inc.count += inc.step

        def wrapped(e, fn=fn, s=inc.h, st=inc.step):
            fn(e).then_inc(s, st)
        self.prog[eng].append(wrapped)
        return inc.count

    def check(self):
        for c in self.chans:
            assert c.max_waited <= c.count, (
                f"channel {c.name}: waited {c.max_waited} > produced {c.count}")


def build_nc(stages=("ffn", "ssm", "ffn")):
    nc = bass.Bass("TRN2", target_bir_lowering=False)
    es = ExitStack()

    din = lambda n, s: nc.dram_tensor(n, s, F32, kind="ExternalInput").ap()
    xT = din("xT", [D, T])
    W1 = din("W1", [D, F])
    W2 = din("W2", [F, D])
    BwT = din("BwT", [D, 128])
    CwT = din("CwT", [128, D])
    DwT = din("DwT", [D, D])
    b1t_in = din("b1t", [128, 32])
    b2t_in = din("b2t", [128, 8])
    tht_in = din("tht", [128, 32])
    wrt_in = din("wrt", [128, 32])
    wit_in = din("wit", [128, 32])
    laR_in = din("laR", [128, 1])
    laI_in = din("laI", [128, 1])
    OUT = nc.dram_tensor("out", [D, T], F32, kind="ExternalOutput").ap()
    dscr = lambda n, s: nc.dram_tensor(n, s, BF16, kind="Internal").ap()
    W1S = [dscr("W1S_hi", [NFC, 128, DK * FC]), dscr("W1S_lo", [NFC, 128, DK * FC])]
    W2S = [dscr("W2S_hi", [DK, 128, F]), dscr("W2S_lo", [DK, 128, F])]
    DS = [dscr("DS_hi", [DK, 128, D]), dscr("DS_lo", [DK, 128, D])]

    sb = lambda n, s, d=F32: es.enter_context(nc.sbuf_tensor(n, s, d)).ap()
    x = sb("x", [128, DK * T])
    ws = sb("ws", [128, 24576])
    xnhi = sb("xnhi", [128, DK * TT], BF16)
    xnlo = sb("xnlo", [128, DK * TT], BF16)
    bwt_h = sb("bwt_h", [128, DK * 128], BF16)
    bwt_l = sb("bwt_l", [128, DK * 128], BF16)
    cwt_h = sb("cwt_h", [128, D], BF16)
    cwt_l = sb("cwt_l", [128, D], BF16)
    stat = sb("stat", [128, 1024])
    x2t = sb("x2t", [128, 1024])
    csl = sb("csl", [128, 1024])
    ssl = sb("ssl", [128, 1024])
    tqa = sb("tqa", [128, 512])
    tqb = sb("tqb", [128, 512])
    tbl = sb("tbl", [128, 256])
    one128 = sb("one128", [128, 1])
    one1 = sb("one1", [1, 128])

    b1t = tbl[:, 0:32]
    K1t = tbl[:, 32:64]
    K2t = tbl[:, 64:96]
    biasC = tbl[:, 96:128]
    biasS = tbl[:, 128:160]
    b2h = tbl[:, 160:168]
    aR = tbl[:, 168:169]; aI = tbl[:, 169:170]; absA = tbl[:, 170:171]
    phi2p = tbl[:, 171:172]
    t0_ = tbl[:, 172:173]; t1_ = tbl[:, 173:174]; t2_ = tbl[:, 174:175]
    t3_ = tbl[:, 175:176]; t4_ = tbl[:, 176:177]
    tht = tbl[:, 177:209]
    wrt = tbl[:, 209:241]
    wit = csl[:, 0:32]
    kqt = csl[:, 32:64]
    sth = csl[:, 64:96]
    cth = csl[:, 96:128]
    tg1 = ssl[:, 0:32]
    tg2 = ssl[:, 32:64]
    tg3 = ssl[:, 64:96]

    wsf = lambda a, b: ws[:, a:b]
    wsb = lambda a, b: ws[:, a:b].bitcast(BF16)
    qhi = wsb(0, 8192)
    qlo = wsb(8192, 16384)
    w1b = [wsb(16384 + 1024 * i, 17408 + 1024 * i) for i in range(4)]
    w2b = [wsb(20480 + 1024 * i, 21504 + 1024 * i) for i in range(4)]
    xn2hi, xn2lo = qhi, qlo
    Bu = wsf(16384, 18432)
    u_ = wsf(18432, 20480)
    g_ = wsf(20480, 22528)
    gX = wsf(22528, 24576)
    BuX = g_
    rotmp = u_
    h_ = Bu
    hhi = wsb(18432, 19456)
    hlo = wsb(19456, 20480)
    dd = [wsb(20480, 21504), wsb(21504, 22528)]
    cs_c = xnhi.bitcast(F32)
    cs_s = xnlo.bitcast(F32)
    stg = [wsf(0, 4096), wsf(4096, 8192)]
    ohi = [wsb(16384, 18432), wsb(20480, 22528)]
    olo = [wsb(18432, 20480), wsb(22528, 24576)]

    psv = [es.enter_context(nc.psum_tensor(f"p{i}", [128, 512], F32)).ap() for i in range(8)]

    P = Prog()
    ch = lambda n, s: P.chans.append(Ch(nc, es, n, s)) or P.chans[-1]
    c_sd = ch("c_sd", 16)
    c_gd = ch("c_gd", 16)
    c_tn = ch("c_tn", 1)
    c_vc = ch("c_vc", 1)
    c_ac = ch("c_ac", 1)
    c_xd = ch("c_xd", 16)
    W, E = P.wait, P.do

    bank_free = {i: 0 for i in range(8)}

    for (dst, src) in ((b1t, b1t_in), (b2h, b2t_in), (tht, tht_in), (wrt, wrt_in),
                       (wit, wit_in), (aR, laR_in), (aI, laI_in)):
        E("scalar", lambda e, d=dst, s=src: e.dma_start(out=d, in_=s[:, :]), inc=c_xd)
    small_loaded = c_xd.count
    x_loaded_t = []
    for t in range(NTT):
        for dk in range(DK):
            E("scalar", lambda e, dk=dk, t=t: e.dma_start(
                out=x[:, dk * T + t * TT:dk * T + (t + 1) * TT],
                in_=xT[128 * dk:128 * (dk + 1), t * TT:(t + 1) * TT]), inc=c_xd)
        x_loaded_t.append(c_xd.count)
    x_loaded = c_xd.count

    E("vector", lambda e: e.memset(one128[:, :], 1.0))
    E("vector", lambda e: e.memset(one1[:, :], 1.0), inc=c_vc)
    ones_ready = c_vc.count

    W("scalar", c_xd, small_loaded)
    E("scalar", lambda e: e.activation(t0_, aR, Act.Exp), inc=c_ac)
    W("vector", c_ac)
    E("vector", lambda e: e.tensor_scalar(t0_, t0_, -0.5, None, Alu.mult))
    E("vector", lambda e: e.drain())
    E("vector", lambda e: e.tensor_scalar(t1_, aI, 0.5, None, Alu.mult))
    E("vector", lambda e: e.drain())
    E("vector", lambda e: e.tensor_scalar(t2_, t0_, -1.0, 1.0, Alu.mult, Alu.add))
    E("vector", lambda e: e.drain())
    E("vector", lambda e: e.tensor_tensor(t2_, t2_, t2_, Alu.mult))
    E("vector", lambda e: e.drain())
    E("vector", lambda e: e.tensor_tensor(t3_, t1_, t1_, Alu.mult))
    E("vector", lambda e: e.drain())
    E("vector", lambda e: e.tensor_tensor(t2_, t2_, t3_, Alu.add))
    E("vector", lambda e: e.drain())
    E("vector", lambda e: e.reciprocal(t2_, t2_))
    E("vector", lambda e: e.drain())
    E("vector", lambda e: e.tensor_tensor(t4_, t0_, t0_, Alu.mult))
    E("vector", lambda e: e.drain())
    E("vector", lambda e: e.tensor_tensor(t4_, t4_, t3_, Alu.add))
    E("vector", lambda e: e.drain())
    E("vector", lambda e: e.tensor_scalar(t4_, t4_, -1.0, 1.0, Alu.mult, Alu.add))
    E("vector", lambda e: e.drain())
    E("vector", lambda e: e.tensor_tensor(aR, t4_, t2_, Alu.mult))
    E("vector", lambda e: e.drain())
    E("vector", lambda e: e.tensor_tensor(t1_, t1_, t2_, Alu.mult))
    E("vector", lambda e: e.drain())
    E("vector", lambda e: e.tensor_scalar(aI, t1_, 2.0, None, Alu.mult))
    E("vector", lambda e: e.drain())
    E("vector", lambda e: e.tensor_tensor(t0_, aR, aR, Alu.mult))
    E("vector", lambda e: e.drain())
    E("vector", lambda e: e.tensor_tensor(t1_, aI, aI, Alu.mult))
    E("vector", lambda e: e.drain())
    E("vector", lambda e: e.tensor_tensor(t0_, t0_, t1_, Alu.add), inc=c_vc)
    E("vector", lambda e: e.drain())
    W("scalar", c_vc)
    E("scalar", lambda e: e.activation(absA, t0_, Act.Sqrt), inc=c_ac)
    W("vector", c_ac)
    E("vector", lambda e: e.reciprocal(t0_, aR))
    E("vector", lambda e: e.drain())
    E("vector", lambda e: e.tensor_tensor(t1_, aI, t0_, Alu.mult), inc=c_vc)
    E("vector", lambda e: e.drain())
    W("scalar", c_vc)
    E("scalar", lambda e: e.activation(t2_, t1_, Act.Arctan), inc=c_ac)
    E("scalar", lambda e: e.activation(t3_, aI, Act.Sign), inc=c_ac)
    W("vector", c_ac)
    E("vector", lambda e: e.tensor_scalar(t4_, aR, 0.0, None, Alu.is_lt))
    E("vector", lambda e: e.drain())
    E("vector", lambda e: e.tensor_tensor(t4_, t4_, t3_, Alu.mult))
    E("vector", lambda e: e.drain())
    E("vector", lambda e: e.scalar_tensor_tensor(t2_, t4_, math.pi, t2_, Alu.mult, Alu.add))
    E("vector", lambda e: e.drain())
    E("vector", lambda e: e.tensor_scalar(phi2p, t2_, 1.0 / TWO_PI, None, Alu.mult), inc=c_vc)
    E("vector", lambda e: e.drain())

    def round_reduce(dst, srcf):
        tgi = tg1.bitcast(I32)
        E("vector", lambda e: e.tensor_scalar(tg2, srcf, 1.0, 16384.0, Alu.mult, Alu.add))
        E("vector", lambda e: e.drain())
        E("vector", lambda e: e.tensor_copy(tgi, tg2))
        E("vector", lambda e: e.drain())
        E("vector", lambda e: e.tensor_copy(tg2, tgi))
        E("vector", lambda e: e.drain())
        E("vector", lambda e: e.tensor_scalar(tg2, tg2, 1.0, -16384.0, Alu.mult, Alu.add))
        E("vector", lambda e: e.drain())
        E("vector", lambda e: e.scalar_tensor_tensor(tg2, tg2, -1.0, srcf,
                                                     Alu.mult, Alu.add))
        E("vector", lambda e: e.drain())
        E("vector", lambda e: e.tensor_scalar(dst, tg2, TWO_PI, None, Alu.mult))
        E("vector", lambda e: e.drain())

    W("vector", c_xd, small_loaded)
    E("vector", lambda e: e.tensor_scalar(tg3, tht, 1.0 / TWO_PI, None, Alu.mult))
    E("vector", lambda e: e.drain())
    round_reduce(sth, tg3)
    E("vector", lambda e: e.tensor_scalar(tg3, tg3, 1.0, 0.25, Alu.mult, Alu.add))
    E("vector", lambda e: e.drain())
    round_reduce(cth, tg3)
    E("vector", lambda e: e.nop(), inc=c_vc)
    E("vector", lambda e: e.drain())
    W("scalar", c_vc)
    E("scalar", lambda e: e.activation(sth, sth, Act.Sin))
    E("scalar", lambda e: e.activation(cth, cth, Act.Sin), inc=c_ac)
    W("vector", c_ac)
    E("vector", lambda e: e.tensor_tensor(tg1, cth, wrt, Alu.mult))
    E("vector", lambda e: e.drain())
    E("vector", lambda e: e.tensor_tensor(tg2, sth, wit, Alu.mult))
    E("vector", lambda e: e.drain())
    E("vector", lambda e: e.tensor_tensor(tg1, tg1, tg2, Alu.add))
    E("vector", lambda e: e.drain())
    E("vector", lambda e: e.tensor_tensor(tg1, tg1, wit, Alu.mult))
    E("vector", lambda e: e.drain())
    E("vector", lambda e: e.tensor_scalar(kqt, tg1, 0.1, None, Alu.mult))
    E("vector", lambda e: e.drain())
    E("vector", lambda e: e.tensor_tensor(K1t, kqt, sth, Alu.mult))
    E("vector", lambda e: e.drain())
    E("vector", lambda e: e.tensor_tensor(K2t, kqt, cth, Alu.mult))
    E("vector", lambda e: e.drain())
    E("vector", lambda e: e.tensor_scalar(biasC, b1t, 0.1, math.pi / 2, Alu.mult, Alu.add))
    E("vector", lambda e: e.drain())
    E("vector", lambda e: e.tensor_scalar(biasS, b1t, 0.1, None, Alu.mult))
    E("vector", lambda e: e.drain())
    E("vector", lambda e: e.tensor_scalar(b2h, b2h, 0.5, None, Alu.mult), inc=c_vc)
    E("vector", lambda e: e.drain())
    qtables_done = c_vc.count

    units = []
    for fc in range(NFC):
        src = W1[:, fc * FC:(fc + 1) * FC].rearrange("(dk p) f -> p dk f", p=128)
        units.append(("w1", fc, src, DK * FC, DK))
    for dk in range(DK):
        src = W2[:, dk * 128:(dk + 1) * 128].rearrange("(fs p) d -> p fs d", p=128)
        units.append(("w2", dk, src, F, FK))
    units.append(("bwt", 0, BwT[:, :].rearrange("(dk p) n -> p dk n", p=128), DK * 128, DK))
    units.append(("cwt", 0, CwT[:, :].rearrange("n (a d2) -> n a d2", a=1), D, 1))
    for ek in range(DK):
        src = DwT[:, ek * 128:(ek + 1) * 128].rearrange("(dk p) e2 -> p dk e2", p=128)
        units.append(("dwt", ek, src, D, DK))

    vec_split = {}
    wb_done = {}
    for i, (kind, idx, src, n, mid) in enumerate(units):
        s = i % 2
        if i >= 2:
            W("sync", c_vc, vec_split[i - 2])
        E("sync", lambda e, src=src, s=s, n=n, mid=mid: e.dma_start(
            out=stg[s][:, 0:n].rearrange("p (a b) -> p a b", a=mid), in_=src),
          inc=c_sd)
        ld = c_sd.count
        W("vector", c_sd, ld)
        if i >= 2 and units[i - 2][0] not in ("bwt", "cwt"):
            W("vector", c_gd, wb_done[i - 2])
        if kind == "bwt":
            dsth, dstl = bwt_h[:, :], bwt_l[:, :]
        elif kind == "cwt":
            dsth, dstl = cwt_h[:, :], cwt_l[:, :]
        else:
            dsth, dstl = ohi[s][:, 0:n], olo[s][:, 0:n]
        E("vector", lambda e, d=dsth, s=s, n=n: e.tensor_copy(d, stg[s][:, 0:n]))
        E("vector", lambda e, dh=dsth, dl=dstl, s=s, n=n:
          e.scalar_tensor_tensor(dl, dh, -1.0, stg[s][:, 0:n], Alu.mult, Alu.add),
          inc=c_vc)
        vec_split[i] = c_vc.count
        if kind not in ("bwt", "cwt"):
            W("gpsimd", c_vc, vec_split[i])
            scr = {"w1": W1S, "w2": W2S, "dwt": DS}[kind]
            outs = (scr[0][idx], scr[1][idx])
            E("gpsimd", lambda e, o=outs[0], s=s, n=n: e.dma_start(out=o[:, :],
                                                                   in_=ohi[s][:, 0:n]), inc=c_gd)
            E("gpsimd", lambda e, o=outs[1], s=s, n=n: e.dma_start(out=o[:, :],
                                                                   in_=olo[s][:, 0:n]), inc=c_gd)
            wb_done[i] = c_gd.count
    w1_ready = [wb_done[i] for i, un in enumerate(units) if un[0] == "w1"]
    w2_ready = [wb_done[i] for i, un in enumerate(units) if un[0] == "w2"]
    d_ready = [wb_done[i] for i, un in enumerate(units) if un[0] == "dwt"]
    bwt_ready = [vec_split[i] for i, un in enumerate(units) if un[0] == "bwt"][0]
    cwt_ready = [vec_split[i] for i, un in enumerate(units) if un[0] == "cwt"][0]
    startup_sd_all = c_sd.count
    startup_gd_all = c_gd.count

    w1slot_free = [0, 0]
    w2slot_free = [0, 0]
    ddslot_free = [0, 0]
    sqslot_free = [0, 0]
    csslot_free = [0, 0]
    state = {"last_split_wb": c_gd.count}

    def ln_stats(t, wait_scalar_x, sq_vector=False):
        ts_ = t * TT
        mslot = stat[0:1, 0:512]
        rslot = stat[0:1, 512:1024]
        tmp1 = tqa[0:1, 0:512]
        W("tensor", c_xd, x_loaded_t[t])
        W("tensor", c_vc, max(bank_free[6], bank_free[7]))
        for dk in range(DK):
            s = dk % 2
            sqe = "vector" if sq_vector else "scalar"
            W(sqe, c_tn, sqslot_free[s])
            W(sqe, c_vc, wait_scalar_x)
            W(sqe, c_xd, x_loaded_t[t])
            if sq_vector:
                E("vector", lambda e, dk=dk, s=s: e.tensor_mul(
                    x2t[:, s * 512:(s + 1) * 512], x[:, dk * T + ts_:dk * T + ts_ + TT],
                    x[:, dk * T + ts_:dk * T + ts_ + TT]), inc=c_vc)
                sqch, sqv = c_vc, c_vc.count
            else:
                E("scalar", lambda e, dk=dk, s=s: e.activation(
                    x2t[:, s * 512:(s + 1) * 512], x[:, dk * T + ts_:dk * T + ts_ + TT],
                    Act.Square), inc=c_ac)
                sqch, sqv = c_ac, c_ac.count
            E("tensor", lambda e, dk=dk: e.matmul(
                psv[6][0:1, :], one128[:, :], x[:, dk * T + ts_:dk * T + ts_ + TT],
                start=(dk == 0), stop=(dk == DK - 1), skip_group_check=True))
            W("tensor", sqch, sqv)
            E("tensor", lambda e, s=s, dk=dk: e.matmul(
                psv[7][0:1, :], one128[:, :], x2t[:, s * 512:(s + 1) * 512],
                start=(dk == 0), stop=(dk == DK - 1), skip_group_check=True), inc=c_tn)
            sqslot_free[s] = c_tn.count
        stats_mm = c_tn.count
        W("vector", c_tn, stats_mm)
        E("vector", lambda e, m=mslot: e.tensor_scalar(m, psv[6][0:1, :], 1.0 / D, None, Alu.mult))
        E("vector", lambda e, r=rslot: e.tensor_scalar(r, psv[7][0:1, :], 1.0 / D, None, Alu.mult))
        E("vector", lambda e, m=mslot: e.tensor_tensor(tmp1, m, m, Alu.mult))
        E("vector", lambda e, r=rslot: e.tensor_tensor(r, r, tmp1, Alu.subtract))
        E("vector", lambda e, r=rslot: e.tensor_scalar(r, r, EPS, None, Alu.add), inc=c_vc)
        W("scalar", c_vc)
        E("scalar", lambda e, r=rslot: e.activation(r, r, Act.Sqrt), inc=c_ac)
        W("vector", c_ac)
        E("vector", lambda e, r=rslot: e.reciprocal(r, r), inc=c_vc)
        return c_vc.count

    def ln_apply(t, r_done, xn_hi_dk, xn_lo_dk, wait_vec_xnwrite):
        ts_ = t * TT
        mslot = stat[0:1, 0:512]
        rslot = stat[0:1, 512:1024]
        W("tensor", c_vc, r_done)
        E("tensor", lambda e, m=mslot: e.matmul(psv[6][:, :], one1[:, :], m,
                                                start=True, stop=True, skip_group_check=True))
        E("tensor", lambda e, r=rslot: e.matmul(psv[7][:, :], one1[:, :], r,
                                                start=True, stop=True, skip_group_check=True),
          inc=c_tn)
        bcast_done = c_tn.count
        W("vector", c_tn, bcast_done)
        W("vector", c_tn, wait_vec_xnwrite)
        for dk in range(DK):
            E("vector", lambda e, dk=dk: e.tensor_tensor(
                tqa[:, :], x[:, dk * T + ts_:dk * T + ts_ + TT], psv[6][:, :], Alu.subtract))
            E("vector", lambda e: e.tensor_tensor(tqa[:, :], tqa[:, :], psv[7][:, :], Alu.mult))
            E("vector", lambda e, dk=dk: e.tensor_copy(xn_hi_dk(dk), tqa[:, :]))
            E("vector", lambda e, dk=dk: e.scalar_tensor_tensor(
                xn_lo_dk(dk), xn_hi_dk(dk), -1.0, tqa[:, :], Alu.mult, Alu.add), inc=c_vc)
        bank_free[6] = c_vc.count
        bank_free[7] = c_vc.count
        return c_vc.count

    def emit_ln(t, xn_hi_dk, xn_lo_dk, wait_vec_xnwrite, wait_scalar_x):
        r_done = ln_stats(t, wait_scalar_x)
        return ln_apply(t, r_done, xn_hi_dk, xn_lo_dk, wait_vec_xnwrite)

    def emit_ffn(tables_hook=None):
        first_call = state["first_ffn"]
        state["first_ffn"] = False
        xn_hi_dk = lambda dk: xnhi[:, dk * TT:(dk + 1) * TT]
        xn_lo_dk = lambda dk: xnlo[:, dk * TT:(dk + 1) * TT]
        if state.get("pre_xn0") is not None:
            xn_ready = state.pop("pre_xn0")
        else:
            xn_ready = emit_ln(0, xn_hi_dk, xn_lo_dk,
                               state["xn_reader"], state["x_done"])
        if first_call:
            W("sync", c_gd, startup_gd_all)
            W("vector", c_sd, startup_sd_all)
            W("vector", c_gd, startup_gd_all)
        W("sync", c_tn, state["xn_reader"])
        for t in range(NTT):
            ts_ = t * TT
            a_first_group = None
            for fc in range(NFC):
                s = fc % 2
                W("sync", c_tn, w1slot_free[s])
                W("sync", c_gd, w1_ready[fc])
                W("sync", c_vc, xn_ready)
                E("sync", lambda e, fc=fc, s=s: e.dma_start(out=w1b[2 * s][:, :],
                                                            in_=W1S[0][fc][:, :]), inc=c_sd)
                E("sync", lambda e, fc=fc, s=s: e.dma_start(out=w1b[2 * s + 1][:, :],
                                                            in_=W1S[1][fc][:, :]), inc=c_sd)
                w1_ld = c_sd.count
                for fs in range(2):
                    sidx = fc * 2 + fs
                    bank = 2 + (sidx % 4)
                    cslot = sidx % 2
                    W("tensor", c_sd, w1_ld)
                    W("tensor", c_vc, xn_ready)
                    W("tensor", c_vc, bank_free[bank])
                    for p in range(3):
                        lh = w1b[2 * s] if p < 2 else w1b[2 * s + 1]
                        rh = xn_hi_dk if p != 1 else xn_lo_dk
                        for dk in range(DK):
                            lhs = lh[:, dk * FC + fs * 128:dk * FC + fs * 128 + 128]
                            first = (p == 0 and dk == 0)
                            last = (p == 2 and dk == DK - 1)
                            E("tensor", lambda e, lhs=lhs, r=rh(dk), b=bank, f=first, l=last:
                              e.matmul(psv[b][:, :], lhs, r, start=f, stop=l,
                                       skip_group_check=True),
                              inc=(c_tn if last else None))
                    mm_done = c_tn.count
                    if a_first_group is None:
                        a_first_group = mm_done
                    w1slot_free[s] = mm_done
                    W("scalar", c_tn, mm_done)
                    W("scalar", c_vc, csslot_free[cslot])
                    E("scalar", lambda e, b=bank, c2=cslot, i=sidx: e.activation(
                        csl[:, c2 * 512:(c2 + 1) * 512], psv[b][:, :], Act.Sin,
                        bias=biasC[:, i:i + 1], scale=0.1))
                    E("scalar", lambda e, b=bank, c2=cslot, i=sidx: e.activation(
                        ssl[:, c2 * 512:(c2 + 1) * 512], psv[b][:, :], Act.Sin,
                        bias=biasS[:, i:i + 1], scale=0.1), inc=c_ac)
                    sins_done = c_ac.count
                    W("vector", c_ac, sins_done)
                    cap = csl[:, cslot * 512:(cslot + 1) * 512]
                    sap = ssl[:, cslot * 512:(cslot + 1) * 512]
                    E("vector", lambda e, cap=cap, i=sidx: e.tensor_scalar(
                        tqa[:, :], cap, K1t[:, i:i + 1], b1t[:, i:i + 1], Alu.mult, Alu.add))
                    E("vector", lambda e, sap=sap, i=sidx: e.scalar_tensor_tensor(
                        tqa[:, :], sap, K2t[:, i:i + 1], tqa[:, :], Alu.mult, Alu.add),
                      inc=c_vc)
                    csslot_free[cslot] = c_vc.count
                    E("vector", lambda e, b=bank: e.tensor_tensor(
                        tqb[:, :], tqa[:, :], psv[b][:, :], Alu.add), inc=c_vc)
                    bank_free[bank] = c_vc.count
                    E("vector", lambda e: e.tensor_scalar(tqb[:, :], tqb[:, :], 0.0, None,
                                                          Alu.max))
                    qh = qhi[:, sidx * TT:(sidx + 1) * TT]
                    ql = qlo[:, sidx * TT:(sidx + 1) * TT]
                    E("vector", lambda e, qh=qh: e.tensor_copy(qh, tqb[:, :]))
                    E("vector", lambda e, qh=qh, ql=ql: e.scalar_tensor_tensor(
                        ql, qh, -1.0, tqb[:, :], Alu.mult, Alu.add), inc=c_vc)
            q_ready = c_vc.count
            a_last_mm = c_tn.count
            if t < NTT - 1:
                r_done_next = ln_stats(t + 1, state["x_done"])
            for dk in range(DK):
                bank = dk % 6
                if t < NTT - 1 and dk == 1:
                    xn_ready = ln_apply(t + 1, r_done_next, xn_hi_dk, xn_lo_dk, a_last_mm)
                if t == NTT - 1 and dk == 3 and tables_hook is not None:
                    tables_hook(a_last_mm)
                W("tensor", c_vc, bank_free[bank])
                for h in range(2):
                    un = dk * 2 + h
                    bp = un % 2
                    W("sync", c_tn, w2slot_free[bp])
                    W("sync", c_gd, w2_ready[dk])
                    W("sync", c_vc, state["w2region_guard"])
                    E("sync", lambda e, dk=dk, h=h, bp=bp: e.dma_start(
                        out=w2b[2 * bp][:, :], in_=W2S[0][dk][:, h * 2048:(h + 1) * 2048]),
                      inc=c_sd)
                    E("sync", lambda e, dk=dk, h=h, bp=bp: e.dma_start(
                        out=w2b[2 * bp + 1][:, :], in_=W2S[1][dk][:, h * 2048:(h + 1) * 2048]),
                      inc=c_sd)
                    w2_ld = c_sd.count
                    W("tensor", c_sd, w2_ld)
                    W("tensor", c_vc, q_ready)
                    for fs in range(h * 16, h * 16 + 16):
                        o = (fs - h * 16) * 128
                        for p in range(3):
                            lhs = (w2b[2 * bp] if p < 2 else w2b[2 * bp + 1])[:, o:o + 128]
                            rh = (qhi if p != 1 else qlo)[:, fs * TT:(fs + 1) * TT]
                            first = (h == 0 and fs == 0 and p == 0)
                            stop = (h == 1 and fs == h * 16 + 15 and p == 2)
                            unit_end = (fs == h * 16 + 15 and p == 2)
                            E("tensor", lambda e, lhs=lhs, rh=rh, b=bank, f=first, l=stop:
                              e.matmul(psv[b][:, :], lhs, rh, start=f, stop=l,
                                       skip_group_check=True),
                              inc=(c_tn if unit_end else None))
                    w2slot_free[bp] = c_tn.count
                b_done = c_tn.count
                W("vector", c_tn, b_done)
                E("vector", lambda e, b=bank, dk=dk: e.tensor_scalar(
                    tqb[:, :], psv[b][:, :], 0.5, b2h[:, dk:dk + 1], Alu.mult, Alu.add))
                E("vector", lambda e, dk=dk, ts_=ts_: e.tensor_tensor(
                    x[:, dk * T + ts_:dk * T + ts_ + TT],
                    x[:, dk * T + ts_:dk * T + ts_ + TT], tqb[:, :], Alu.add), inc=c_vc)
                bank_free[bank] = c_vc.count
        state["x_done"] = c_vc.count
        state["xn_reader"] = c_tn.count
        state["w2region_guard"] = 0

    def emit_scan_tables(wait_tn):
        tmp1 = wsf(16384, 18432)
        tmp2 = wsf(18432, 20480)
        W("gpsimd", c_tn, wait_tn)
        E("gpsimd", lambda e: e.iota(tmp1[:, :], [[1, T]], base=0, channel_multiplier=0,
                                     allow_small_or_imprecise_dtypes=True), inc=c_gd)
        iota_done = c_gd.count
        W("vector", c_gd, iota_done)
        W("vector", c_tn, wait_tn)
        csi = xnhi.bitcast(I32)
        E("vector", lambda e: e.tensor_scalar(tmp2[:, :], tmp1[:, :], phi2p[:, 0:1], None,
                                              Alu.mult))
        E("vector", lambda e: e.tensor_scalar(tmp1[:, :], tmp2[:, :], 1.0, 16384.0,
                                              Alu.mult, Alu.add))
        E("vector", lambda e: e.tensor_copy(csi[:, :], tmp1[:, :]))
        E("vector", lambda e: e.tensor_copy(tmp1[:, :], csi[:, :]))
        E("vector", lambda e: e.tensor_scalar(tmp1[:, :], tmp1[:, :], 1.0, -16384.0,
                                              Alu.mult, Alu.add))
        E("vector", lambda e: e.scalar_tensor_tensor(tmp1[:, :], tmp1[:, :], -1.0,
                                                     tmp2[:, :], Alu.mult, Alu.add))
        E("vector", lambda e: e.tensor_scalar(cs_s[:, :], tmp1[:, :], TWO_PI, None,
                                              Alu.mult))
        E("vector", lambda e: e.tensor_scalar(tmp2[:, :], tmp2[:, :], 1.0, 0.25,
                                              Alu.mult, Alu.add))
        E("vector", lambda e: e.tensor_scalar(tmp1[:, :], tmp2[:, :], 1.0, 16384.0,
                                              Alu.mult, Alu.add))
        E("vector", lambda e: e.tensor_copy(csi[:, :], tmp1[:, :]))
        E("vector", lambda e: e.tensor_copy(tmp1[:, :], csi[:, :]))
        E("vector", lambda e: e.tensor_scalar(tmp1[:, :], tmp1[:, :], 1.0, -16384.0,
                                              Alu.mult, Alu.add))
        E("vector", lambda e: e.scalar_tensor_tensor(tmp1[:, :], tmp1[:, :], -1.0,
                                                     tmp2[:, :], Alu.mult, Alu.add))
        E("vector", lambda e: e.tensor_scalar(cs_c[:, :], tmp1[:, :], TWO_PI, None,
                                              Alu.mult),
          inc=c_vc)
        args_ready = c_vc.count
        W("scalar", c_vc, args_ready)
        E("scalar", lambda e: e.activation(cs_s[:, :], cs_s[:, :], Act.Sin))
        E("scalar", lambda e: e.activation(cs_c[:, :], cs_c[:, :], Act.Sin), inc=c_ac)
        state["cs_done"] = c_ac.count
        state["tables_emitted"] = True

    def emit_ssm(dump=False):
        ssm_start_vc = c_vc.count
        xn2_hi_dk = lambda dk, ts_: xn2hi[:, dk * T + ts_:dk * T + ts_ + TT]
        xn2_lo_dk = lambda dk, ts_: xn2lo[:, dk * T + ts_:dk * T + ts_ + TT]

        if not state.get("tables_emitted"):
            emit_scan_tables(state["xn_reader"])
        cs_done = state["cs_done"]

        def bproj(t, xn2_rdy):
            ts_ = t * TT
            bank = t % 2
            W("tensor", c_vc, bank_free[bank])
            W("tensor", c_vc, bwt_ready)
            W("tensor", c_vc, xn2_rdy)
            for p in range(3):
                lh = bwt_h if p < 2 else bwt_l
                rh = xn2_hi_dk if p != 1 else xn2_lo_dk
                for dk in range(DK):
                    first = (p == 0 and dk == 0)
                    last = (p == 2 and dk == DK - 1)
                    E("tensor", lambda e, l2=lh[:, dk * 128:(dk + 1) * 128],
                      r=rh(dk, ts_), b=bank, f=first, l=last:
                      e.matmul(psv[b][:, :], l2, r, start=f, stop=l, skip_group_check=True),
                      inc=(c_tn if last else None))
            bu_mm = c_tn.count
            W("vector", c_tn, bu_mm)
            E("vector", lambda e, b=bank, ts_=ts_: e.tensor_copy(
                Bu[:, ts_:ts_ + TT], psv[b][:, :]), inc=c_vc)
            bank_free[bank] = c_vc.count

        hi0 = lambda t: (lambda dk, ts_=t * TT: xn2_hi_dk(dk, ts_))
        lo0 = lambda t: (lambda dk, ts_=t * TT: xn2_lo_dk(dk, ts_))
        xn_rdy = {}
        r_d = ln_stats(0, state["x_done"], sq_vector=True)
        xn_rdy[0] = ln_apply(0, r_d, hi0(0), lo0(0), state["xn_reader"])
        r_next = ln_stats(1, state["x_done"], sq_vector=True)
        for t in range(NTT):
            bproj(t, xn_rdy[t])
            if t < NTT - 1:
                xn_rdy[t + 1] = ln_apply(t + 1, r_next, hi0(t + 1), lo0(t + 1),
                                         state["xn_reader"])
                if t < NTT - 2:
                    r_next = ln_stats(t + 2, state["x_done"], sq_vector=True)
        bu_all = c_vc.count

        if dump:
            E("vector", lambda e: e.tensor_copy(x[:, 2048:4096], Bu[:, :]), inc=c_vc)
        W("gpsimd", c_vc, bu_all)
        E("gpsimd", lambda e: e.dma_start(out=BuX[0:64, :], in_=Bu[64:128, :]), inc=c_gd)
        E("gpsimd", lambda e: e.dma_start(out=BuX[64:128, :], in_=Bu[0:64, :]), inc=c_gd)
        swap1 = c_gd.count
        W("vector", c_gd, swap1)
        W("vector", c_ac, cs_done)
        E("vector", lambda e: e.tensor_tensor(u_[:, :], Bu[:, :], cs_c[:, :], Alu.mult))
        E("vector", lambda e: e.tensor_tensor(gX[:, :], BuX[:, :], cs_s[:, :], Alu.mult))
        E("vector", lambda e: e.tensor_tensor(u_[0:64, :], u_[0:64, :], gX[0:64, :], Alu.add))
        E("vector", lambda e: e.tensor_tensor(u_[64:128, :], u_[64:128, :], gX[64:128, :],
                                              Alu.subtract))
        E("vector", lambda e: e.tensor_tensor_scan(
            g_[:, :], absA[:, 0:1].broadcast_to((128, T)), u_[:, :], 0.0,
            Alu.mult, Alu.add), inc=c_vc)
        scan_done = c_vc.count
        W("gpsimd", c_vc, scan_done)
        E("gpsimd", lambda e: e.dma_start(out=gX[0:64, :], in_=g_[64:128, :]), inc=c_gd)
        E("gpsimd", lambda e: e.dma_start(out=gX[64:128, :], in_=g_[0:64, :]), inc=c_gd)
        swap2 = c_gd.count
        W("vector", c_gd, swap2)
        E("vector", lambda e: e.tensor_tensor(rotmp[:, :], gX[:, :], cs_s[:, :], Alu.mult))
        E("vector", lambda e: e.tensor_tensor(h_[:, :], g_[:, :], cs_c[:, :], Alu.mult))
        E("vector", lambda e: e.tensor_tensor(h_[0:64, :], h_[0:64, :], rotmp[0:64, :],
                                              Alu.subtract))
        E("vector", lambda e: e.tensor_tensor(h_[64:128, :], h_[64:128, :], rotmp[64:128, :],
                                              Alu.add), inc=c_vc)
        rot_out_done = c_vc.count
        if dump:
            E("vector", lambda e: e.tensor_copy(x[:, 0:2048], h_[:, :]))
            E("vector", lambda e: e.tensor_copy(x[:, 4096:6144], cs_c[:, :]))
            E("vector", lambda e: e.tensor_copy(x[:, 6144:8192], cs_s[:, :]), inc=c_vc)
            state["x_done"] = c_vc.count
            state["xn_reader"] = c_tn.count
            state["w2region_guard"] = c_vc.count
            return
        E("vector", lambda e: e.tensor_copy(hhi[:, :], h_[:, :]))
        E("vector", lambda e: e.scalar_tensor_tensor(hlo[:, :], hhi[:, :], -1.0, h_[:, :],
                                                     Alu.mult, Alu.add), inc=c_vc)
        h_split = c_vc.count

        dd2 = [csl[:, :].bitcast(BF16), ssl[:, :].bitcast(BF16)]
        pairs = [(t, ek) for t in range(NTT) for ek in range(DK)]

        def dpart(i):
            t, ek = pairs[i]
            ts_ = t * TT
            sl = i % 2
            bank = i % 6
            W("sync", c_tn, ddslot_free[sl])
            W("sync", c_vc, ssm_start_vc)
            W("sync", c_gd, d_ready[ek])
            E("sync", lambda e, ek=ek, sl=sl: e.dma_start(
                out=dd2[sl][:, 0:1024], in_=DS[0][ek][:, :]), inc=c_sd)
            E("sync", lambda e, ek=ek, sl=sl: e.dma_start(
                out=dd2[sl][:, 1024:2048], in_=DS[1][ek][:, :]), inc=c_sd)
            d_ld = c_sd.count
            W("tensor", c_vc, bank_free[bank])
            W("tensor", c_sd, d_ld)
            for p in range(3):
                off = 0 if p < 2 else 1024
                rh = xn2_hi_dk if p != 1 else xn2_lo_dk
                for dk in range(DK):
                    E("tensor", lambda e, l2=dd2[sl][:, off + dk * 128:off + (dk + 1) * 128],
                      r=rh(dk, ts_), b=bank, f=(p == 0 and dk == 0):
                      e.matmul(psv[b][:, :], l2, r, start=f, stop=False,
                               skip_group_check=True),
                      inc=(c_tn if (p == 2 and dk == DK - 1) else None))
            ddslot_free[sl] = c_tn.count

        def cpart(i):
            t, ek = pairs[i]
            ts_ = t * TT
            bank = i % 6
            W("tensor", c_vc, max(h_split, cwt_ready))
            for p in range(3):
                lh = cwt_h if p < 2 else cwt_l
                rh2 = hhi if p != 1 else hlo
                E("tensor", lambda e, l2=lh[:, ek * 128:(ek + 1) * 128],
                  r=rh2[:, ts_:ts_ + TT], b=bank, l=(p == 2):
                  e.matmul(psv[b][:, :], l2, r, start=False, stop=l,
                           skip_group_check=True),
                  inc=(c_tn if p == 2 else None))
            cd_mm = c_tn.count
            W("vector", c_tn, cd_mm)
            E("vector", lambda e, b=bank, ek=ek, ts_=ts_: e.tensor_tensor(
                x[:, ek * T + ts_:ek * T + ts_ + TT],
                x[:, ek * T + ts_:ek * T + ts_ + TT], psv[b][:, :], Alu.add), inc=c_vc)
            bank_free[bank] = c_vc.count

        DEPTH = 2
        for i in range(min(DEPTH, len(pairs))):
            dpart(i)
        for i in range(len(pairs)):
            cpart(i)
            if i + DEPTH < len(pairs):
                dpart(i + DEPTH)
            if i == DK - 1:
                r0f = ln_stats(0, c_vc.count)
                state["pre_xn0"] = ln_apply(
                    0, r0f, lambda dk: xnhi[:, dk * TT:(dk + 1) * TT],
                    lambda dk: xnlo[:, dk * TT:(dk + 1) * TT], 0)
        state["x_done"] = c_vc.count
        state["xn_reader"] = c_tn.count
        state["w2region_guard"] = rot_out_done
    state["x_done"] = 0
    csslot_free[0] = qtables_done
    csslot_free[1] = qtables_done
    state["xn_reader"] = 0
    state["w2region_guard"] = 0
    state["first_ffn"] = True
    state["cs_done"] = 0
    state["tables_emitted"] = False
    if stages and stages[0] == "ffn":
        _r0 = ln_stats(0, 0)
        state["pre_xn0"] = ln_apply(0, _r0, lambda dk: xnhi[:, dk * TT:(dk + 1) * TT],
                                    lambda dk: xnlo[:, dk * TT:(dk + 1) * TT], 0)
    first_stage = True
    for st_ in stages:
        if st_ == "ffn":
            hook = emit_scan_tables if (first_stage and "ssm" in stages) else None
            emit_ffn(tables_hook=hook)
            first_stage = False
        elif st_ == "ssm":
            emit_ssm()
        elif st_ == "ssmh":
            emit_ssm(dump=True)
        elif st_ == "ln":
            xn_hi_dk = lambda dk: xnhi[:, dk * TT:(dk + 1) * TT]
            xn_lo_dk = lambda dk: xnlo[:, dk * TT:(dk + 1) * TT]
            for t in range(NTT):
                ts_ = t * TT
                xr = emit_ln(t, xn_hi_dk, xn_lo_dk, state["xn_reader"], state["x_done"])
                for dk in range(DK):
                    E("vector", lambda e, dk=dk, ts_=ts_: e.tensor_tensor(
                        x[:, dk * T + ts_:dk * T + ts_ + TT], xn_hi_dk(dk), xn_lo_dk(dk),
                        Alu.add), inc=c_vc)
                state["xn_reader"] = c_tn.count
                state["x_done"] = c_vc.count
            state["x_done"] = c_vc.count
        elif st_ == "ffnq":
            xn_hi_dk = lambda dk: xnhi[:, dk * TT:(dk + 1) * TT]
            xn_lo_dk = lambda dk: xnlo[:, dk * TT:(dk + 1) * TT]
            xn_ready = emit_ln(0, xn_hi_dk, xn_lo_dk, state["xn_reader"], state["x_done"])
            W("sync", c_gd, startup_gd_all)
            W("vector", c_sd, startup_sd_all)
            W("vector", c_gd, startup_gd_all)
            for fc in range(NFC):
                s = fc % 2
                W("sync", c_tn, w1slot_free[s])
                W("sync", c_gd, w1_ready[fc])
                W("sync", c_vc, xn_ready)
                E("sync", lambda e, fc=fc, s=s: e.dma_start(out=w1b[2 * s][:, :],
                                                            in_=W1S[0][fc][:, :]), inc=c_sd)
                E("sync", lambda e, fc=fc, s=s: e.dma_start(out=w1b[2 * s + 1][:, :],
                                                            in_=W1S[1][fc][:, :]), inc=c_sd)
                w1_ld = c_sd.count
                for fs in range(2):
                    sidx = fc * 2 + fs
                    bank = 2 + (sidx % 4)
                    cslot = sidx % 2
                    W("tensor", c_sd, w1_ld)
                    W("tensor", c_vc, xn_ready)
                    W("tensor", c_vc, bank_free[bank])
                    for p in range(3):
                        lh = w1b[2 * s] if p < 2 else w1b[2 * s + 1]
                        rh = xn_hi_dk if p != 1 else xn_lo_dk
                        for dk in range(DK):
                            lhs = lh[:, dk * FC + fs * 128:dk * FC + fs * 128 + 128]
                            first = (p == 0 and dk == 0)
                            last = (p == 2 and dk == DK - 1)
                            E("tensor", lambda e, lhs=lhs, r=rh(dk), b=bank, f=first, l=last:
                              e.matmul(psv[b][:, :], lhs, r, start=f, stop=l,
                                       skip_group_check=True),
                              inc=(c_tn if last else None))
                    mm_done = c_tn.count
                    w1slot_free[s] = mm_done
                    W("scalar", c_tn, mm_done)
                    W("scalar", c_vc, csslot_free[cslot])
                    E("scalar", lambda e, b=bank, c2=cslot, i=sidx: e.activation(
                        csl[:, c2 * 512:(c2 + 1) * 512], psv[b][:, :], Act.Sin,
                        bias=biasC[:, i:i + 1], scale=0.1))
                    E("scalar", lambda e, b=bank, c2=cslot, i=sidx: e.activation(
                        ssl[:, c2 * 512:(c2 + 1) * 512], psv[b][:, :], Act.Sin,
                        bias=biasS[:, i:i + 1], scale=0.1), inc=c_ac)
                    sins_done = c_ac.count
                    W("vector", c_ac, sins_done)
                    cap = csl[:, cslot * 512:(cslot + 1) * 512]
                    sap = ssl[:, cslot * 512:(cslot + 1) * 512]
                    E("vector", lambda e, cap=cap, i=sidx: e.tensor_scalar(
                        tqa[:, :], cap, K1t[:, i:i + 1], b1t[:, i:i + 1], Alu.mult, Alu.add))
                    E("vector", lambda e, sap=sap, i=sidx: e.scalar_tensor_tensor(
                        tqa[:, :], sap, K2t[:, i:i + 1], tqa[:, :], Alu.mult, Alu.add),
                      inc=c_vc)
                    csslot_free[cslot] = c_vc.count
                    E("vector", lambda e, b=bank: e.tensor_tensor(
                        tqb[:, :], tqa[:, :], psv[b][:, :], Alu.add), inc=c_vc)
                    bank_free[bank] = c_vc.count
                    E("vector", lambda e: e.tensor_scalar(tqb[:, :], tqb[:, :], 0.0, None,
                                                          Alu.max))
                    E("vector", lambda e, i=sidx: e.tensor_copy(
                        x[:, i * TT:(i + 1) * TT], tqb[:, :]), inc=c_vc)
            state["x_done"] = c_vc.count
        elif st_ == "tables":
            W("vector", c_vc, qtables_done)
            for i, src in enumerate((b1t, K1t, K2t, biasC, biasS)):
                E("vector", lambda e, i=i, src=src: e.tensor_copy(
                    x[:, i * 32:(i + 1) * 32], src), inc=c_vc)
            E("vector", lambda e: e.tensor_copy(x[:, 160:161], absA), inc=c_vc)
            E("vector", lambda e: e.tensor_copy(x[:, 161:162], phi2p), inc=c_vc)
            E("vector", lambda e: e.tensor_copy(x[:, 162:163], aR), inc=c_vc)
            E("vector", lambda e: e.tensor_copy(x[:, 163:164], aI), inc=c_vc)
            state["x_done"] = c_vc.count

    W("sync", c_vc, state["x_done"])
    for dk in range(DK):
        E("sync", lambda e, dk=dk: e.dma_start(out=OUT[128 * dk:128 * (dk + 1), :],
                                               in_=x[:, dk * T:(dk + 1) * T]), inc=c_sd)
    W("sync", c_sd, c_sd.count)

    P.check()
    with es, nc.Block() as block:
        @block.tensor
        def _(e):
            for fn in P.prog["tensor"]:
                fn(e)

        @block.vector
        def _(e):
            for fn in P.prog["vector"]:
                fn(e)

        @block.scalar
        def _(e):
            for fn in P.prog["scalar"]:
                fn(e)

        @block.gpsimd
        def _(e):
            for fn in P.prog["gpsimd"]:
                fn(e)

        @block.sync
        def _(e):
            for fn in P.prog["sync"]:
                fn(e)
    return nc


def kernel(_stages=("ffn", "ssm", "ffn"), **inputs):
    gi = lambda k: np.asarray(inputs[k], dtype=np.float32)
    u = gi("L0_attn_u")
    lar = gi("L0_attn_log_A_real")
    lai = gi("L0_attn_log_A_imag")
    Bw = gi("L0_attn_B_w")
    Cw = gi("L0_attn_C_w")
    Dw = gi("L0_attn_D_w")
    W1 = gi("L0_ffn_W1")
    b1 = gi("L0_ffn_b1")
    th = gi("L0_ffn_quantum_phases")
    wr = gi("L0_ffn_quantum_weights_real")
    wi = gi("L0_ffn_quantum_weights_imag")
    W2 = gi("L0_ffn_W2")
    b2 = gi("L0_ffn_b2")

    C = np.ascontiguousarray
    shared = {
        "W1": C(W1),
        "W2": C(W2),
        "BwT": C(Bw.T),
        "CwT": C(Cw.T),
        "DwT": C(Dw.T),
        "b1t": C(b1.reshape(32, 128).T),
        "b2t": C(b2.reshape(8, 128).T),
        "tht": C(th.reshape(32, 128).T),
        "wrt": C(wr.reshape(32, 128).T),
        "wit": C(wi.reshape(32, 128).T),
        "laR": C(np.concatenate([lar, lar]).reshape(128, 1)),
        "laI": C(np.concatenate([lai, lai]).reshape(128, 1)),
    }
    in_maps = []
    for c in range(8):
        m = dict(shared)
        m["xT"] = C(u[c].T)
        in_maps.append(m)

    nc = build_nc(stages=_stages)
    res = run_bass_kernel_spmd(nc, in_maps, core_ids=list(range(8)))
    out = np.stack([np.asarray(r["out"]).T for r in res.results])
    return np.ascontiguousarray(out).astype(np.float32)


if __name__ == "__main__":
    import reference as _r
    inp = _r.setup_inputs()
    out = kernel(**{k: np.asarray(v) for k, v in inp.items()})
    ref = np.asarray(_r.reference(**inp))
    err = np.linalg.norm(out - ref) / np.linalg.norm(ref)
    print("rel_err:", err)
